# version 19
# baseline (speedup 1.0000x reference)
import numpy as np
import ml_dtypes
import concourse.bass as bass
import concourse.mybir as mybir
from concourse.bass_utils import run_bass_kernel_spmd

F32 = mybir.dt.float32
BF16 = mybir.dt.bfloat16
AF = mybir.ActivationFunctionType
ALU = mybir.AluOpType
AX = mybir.AxisListType

D = 2048
T = 2048
NT = 4
TTK = 512
KCH = 16
EPS_RMS = 1.1920929e-07
EPS_LN = 1e-5
C1 = float(1.0 / (64.0 * np.sqrt(2048.0)))
C2 = float(1.0 / (64.0 * np.sqrt(8192.0)))

ENGS = ("sync", "tensor", "vector", "scalar", "gpsimd")

_ALLOC_ID = [0]


def _stack_alloc(nc, ph):
    def S(name, shape, dt):
        _ALLOC_ID[0] += 1
        return ph.enter_context(nc.sbuf_tensor(f"{name}_{_ALLOC_ID[0]}", shape, dt))

    def P(name, shape, dt):
        _ALLOC_ID[0] += 1
        return ph.enter_context(nc.psum_tensor(f"{name}_{_ALLOC_ID[0]}", shape, dt))
    return S, P


def _had(n):
    h = np.array([[1.0]], np.float32)
    while h.shape[0] < n:
        h = np.block([[h, h], [h, -h]])
    return h


class KB:

    def __init__(self, nc):
        self.nc = nc
        self.sems = {}
        self.cnt = {n: 0 for n in ("PH", "LD", "LW0", "LW1", "LQ", "LP0", "LP1", "SPI", "SQ0", "SQ1", "SM0", "SM1", "SO0", "SO1", "LC", "LC1", "SX0", "SX1", "LQ1", "PO", "PE", "DV", "AC")}
        self.streams = None

    AUTO = {"vector": "DV", "scalar": "AC", "gpsimd": "PO"}

    def q(self, eng, meth, *args, w=(), i=None, **kw):
        if eng in self.AUTO and meth not in ("_wait_only_", "dma_start"):
            sn = self.AUTO[eng]
            w = tuple(w) + ((sn, self.cnt[sn]),)
            if i is None:
                i = (sn, 1)
        if i is not None:
            self.cnt[i[0]] += i[1]
        v = self.cnt[i[0]] if i is not None else None
        w = tuple((sn, val) for sn, val in w if val)
        self.streams[eng].append((meth, args, kw, w, i))
        return v

    def dma(self, eng, out, in_, sem="LD", w=()):
        return self.q(eng, "dma_start", w=w, i=(sem, 16), out=out, in_=in_)

    def _replay(self, eng, h):
        for meth, args, kw, w, i in self.streams[eng]:
            for sn, val in w:
                h.wait_ge(self.sems[sn], val)
            if meth == "_wait_only_":
                continue
            r = getattr(h, meth)(*args, **kw)
            if i is not None:
                r.then_inc(self.sems[i[0]], i[1])

    def new_phase(self):
        self.streams = {e: [] for e in ENGS}

    SPILL_SEMS = ("SPI", "SQ0", "SQ1", "SM0", "SM1", "SO0", "SO1", "SX0", "SX1")

    def run_phase(self):
        self.cnt["PH"] += 5
        tgt = self.cnt["PH"]
        spw = tuple((sn, self.cnt[sn]) for sn in self.SPILL_SEMS if self.cnt[sn])
        self.streams["gpsimd"].append(("_wait_only_", (), {}, spw, None))
        for e in ENGS:
            self.streams[e].append(("sem_inc", None, {}, (), None))
            self.streams[e].append(("_wait_only_", (), {}, (("PH", tgt),), None))
        with self.nc.Block() as blk:
            for e, fn in (("sync", blk.sync), ("tensor", blk.tensor),
                          ("vector", blk.vector), ("scalar", blk.scalar),
                          ("gpsimd", blk.gpsimd)):
                def body(h, e=e):
                    for meth, args, kw, w, i in self.streams[e]:
                        for sn, val in w:
                            h.wait_ge(self.sems[sn], val)
                        if meth == "_wait_only_":
                            continue
                        if meth == "sem_inc" and args is None:
                            h.sem_inc(self.sems["PH"], 1)
                            continue
                        r = getattr(h, meth)(*args, **kw)
                        if i is not None:
                            r.then_inc(self.sems[i[0]], i[1])
                fn(body)
        self.streams = None


UPTO = 99
F_ITERS = 99
F_PARTS = 99
F_EVAC = 99


class _StopBuild(Exception):
    pass


def _chk(nc, k):
    if k >= UPTO:
        raise _StopBuild(nc)


def build(dbg=False):
    try:
        return _build(dbg)
    except _StopBuild as e:
        return e.args[0]


def _build(dbg=False):
    nc = bass.Bass("TRN2", target_bir_lowering=False)
    kb = KB(nc)
    skind = "ExternalOutput" if dbg else "Internal"
    ap = lambda t: t.ap()

    x_own = ap(nc.dram_tensor("x_own", [D, T], F32, kind="ExternalInput"))
    x_for = ap(nc.dram_tensor("x_for", [3, D, T], F32, kind="ExternalInput"))
    cmask = ap(nc.dram_tensor("cmask", [128, 4], F32, kind="ExternalInput"))
    wqkvT = ap(nc.dram_tensor("wqkvT", [48, 128, KCH, 128], F32, kind="ExternalInput"))
    wswT = ap(nc.dram_tensor("wswT", [128, 128, KCH, 128], F32, kind="ExternalInput"))
    wugT = ap(nc.dram_tensor("wugT", [128, 128, KCH, 128], F32, kind="ExternalInput"))
    woutT = ap(nc.dram_tensor("woutT", [16, 128, 64, 128], F32, kind="ExternalInput"))
    wdownT = ap(nc.dram_tensor("wdownT", [16, 128, 64, 128], F32, kind="ExternalInput"))
    su_ug2 = ap(nc.dram_tensor("su_ug2", [128, 16], F32, kind="ExternalInput"))
    wscu2 = ap(nc.dram_tensor("wscu2", [128, 64], F32, kind="ExternalInput"))
    wscg2 = ap(nc.dram_tensor("wscg2", [128, 64], F32, kind="ExternalInput"))
    wscdp2 = ap(nc.dram_tensor("wscdp2", [128, 16], F32, kind="ExternalInput"))
    su_dp2 = ap(nc.dram_tensor("su_dp2", [128, 64], F32, kind="ExternalInput"))
    h128in = ap(nc.dram_tensor("h128in", [128, 128], BF16, kind="ExternalInput"))
    out_fm = ap(nc.dram_tensor("out_fm", [D, T], F32, kind="ExternalOutput"))

    xqf_sp = ap(nc.dram_tensor("xqf_sp", [3, NT, 128, KCH * TTK], BF16, kind=skind))
    wf_sp = ap(nc.dram_tensor("wf_sp", [3, NT, 128, KCH * TTK], BF16, kind=skind))
    q_sp = ap(nc.dram_tensor("q_sp", [KCH, NT, 128, TTK], BF16, kind=skind))
    xh_sp = ap(nc.dram_tensor("xh_sp", [NT, 128, KCH * TTK], BF16, kind=skind))
    mid_sp = ap(nc.dram_tensor("mid_sp", [64, NT, 128, TTK], BF16, kind=skind))
    psw_sp = ap(nc.dram_tensor("psw_sp", [64, NT, 128, TTK], BF16, kind=skind))
    wdh_sp = ap(nc.dram_tensor("wdh_sp", [16, 128, 64 * 128], BF16, kind=skind))
    proj_sp = ap(nc.dram_tensor("proj_sp", [16, NT, 128, TTK], F32, kind=skind))
    rows_sp = ap(nc.dram_tensor("rows_sp", [3, T], F32, kind=skind))
    cw_sp = ap(nc.dram_tensor("cw_sp", [128, 32], F32, kind=skind)) if dbg else None

    from contextlib import ExitStack
    es = ExitStack()
    for sname in ("PH", "LD", "LW0", "LW1", "LQ", "LP0", "LP1", "SPI", "SQ0",
                  "SQ1", "SM0", "SM1", "SO0", "SO1", "LC", "LC1", "SX0", "SX1", "LQ1", "PO",
                  "PE", "DV", "AC"):
        kb.sems[sname] = es.enter_context(nc.semaphore(sname))

    A_big = es.enter_context(nc.sbuf_tensor("A_big", [128, 65536], BF16))
    h128su = es.enter_context(nc.sbuf_tensor("h128su", [128, 16 * 128], BF16))
    h128sb = es.enter_context(nc.sbuf_tensor("h128sb", [128, 128], BF16))
    ones1 = es.enter_context(nc.sbuf_tensor("ones1", [1, 128], F32))
    onescol = es.enter_context(nc.sbuf_tensor("onescol", [128, 1], BF16))
    epsc = es.enter_context(nc.sbuf_tensor("epsc", [128, 1], F32))
    cmsk = es.enter_context(nc.sbuf_tensor("cmsk", [128, 4], F32))
    combu = es.enter_context(nc.sbuf_tensor("combu", [128, 64], F32))
    combg = es.enter_context(nc.sbuf_tensor("combg", [128, 64], F32))
    combdp = es.enter_context(nc.sbuf_tensor("combdp", [128, 16], F32))
    sudp = es.enter_context(nc.sbuf_tensor("sudp", [128, 64], F32))
    suug = es.enter_context(nc.sbuf_tensor("suug", [128, 16], F32))
    carryW = es.enter_context(nc.sbuf_tensor("carryW", [128, 16], F32))
    carryKV = es.enter_context(nc.sbuf_tensor("carryKV", [128, 16], F32))
    tmpv = es.enter_context(nc.sbuf_tensor("tmpv", [128, 64], F32))

    XQ0, KW0 = 0, 32768

    def xq_tile(c, tt):
        b = XQ0 + c * T + tt * TTK
        return A_big[:, b:b + TTK]

    def kwr_row(j):
        return A_big[:, KW0 + j * T:KW0 + (j + 1) * T]

    def y_row(j):
        return A_big[:, j * T:(j + 1) * T]

    def kwr_tile(j, tt):
        b = KW0 + j * T + tt * TTK
        return A_big[:, b:b + TTK]

    kb.new_phase()
    kb.dma("sync", h128sb[:, :], h128in[:, :])
    kb.dma("sync", cmsk[:, :], cmask[:, :])
    kb.dma("sync", sudp[:, :], su_dp2[:, :])
    kb.dma("sync", suug[:, :], su_ug2[:, :])
    v = kb.dma("sync", tmpv[:, :], wscu2[:, :])
    kb.q("vector", "memset", ones1[:, :], 1.0)
    kb.q("vector", "memset", epsc[:, :], EPS_RMS)
    nc.const_aps.register(F32, EPS_RMS, epsc[:, 0:1])
    kb.q("vector", "memset", onescol[:, :], 1.0)
    kb.q("vector", "memset", carryW[:, :], 0.0)
    kb.q("vector", "memset", carryKV[:, :], 0.0)
    kb.q("vector", "tensor_single_scalar", combu[:, :], tmpv[:, :], 64.0, ALU.mult,
         w=[("LD", v)])
    dv = kb.q("vector", "tensor_mul", combu[:, :], combu[:, :], sudp[:, :], i=("DV", 1))
    v = kb.dma("sync", tmpv[:, :], wscg2[:, :], w=[("DV", dv)])
    dv = kb.q("vector", "tensor_single_scalar", combg[:, :], tmpv[:, :], 64.0, ALU.mult,
              w=[("LD", v)], i=("DV", 1))
    v = kb.dma("sync", tmpv[:, 0:16], wscdp2[:, :], w=[("DV", dv)])
    kb.q("vector", "tensor_single_scalar", combdp[:, :], tmpv[:, 0:16], 64.0, ALU.mult,
         w=[("LD", v)])
    for c in range(KCH):
        kb.q("vector", "tensor_scalar_mul", h128su[:, c * 128:(c + 1) * 128],
             h128sb[:, :], suug[:, c:c + 1])
    kb.run_phase()
    _chk(nc, 0)

    def emit_rows_s1(ps_s, ps_sq, rowa, full=False, tt=None):
        R = lambda k: rowa[:, k * TTK:(k + 1) * TTK]
        kb.q("vector", "tensor_single_scalar", R(0), ps_s[:, :], 1.0 / D, ALU.mult,
             w=[("AC", kb.cnt["AC"]), ("SPI", kb.cnt["SPI"])])
        kb.q("vector", "tensor_single_scalar", R(1), ps_sq[:, :], 1.0 / D, ALU.mult)
        kb.q("vector", "tensor_mul", R(4), R(0), R(0))
        kb.q("vector", "tensor_sub", R(1), R(1), R(4))
        dv = kb.q("vector", "tensor_single_scalar", R(2), R(1), EPS_LN, ALU.add,
                  i=("DV", 1))
        if not full:
            dv = kb.q("vector", "scalar_tensor_tensor", R(1), R(2), EPS_RMS,
                      R(1), ALU.mult, ALU.add, i=("DV", 1))
            ac = kb.q("scalar", "activation", R(5), R(1), AF.Ln,
                      w=[("DV", dv)], i=("AC", 1))
            ac = kb.q("scalar", "activation", R(5), R(5), AF.Exp, scale=-0.5,
                      w=[("AC", ac)], i=("AC", 1))
            kb.q("vector", "tensor_mul", R(0), R(0), R(5), w=[("AC", ac)])
            dv = kb.q("vector", "engine_nop", i=("DV", 1))
            return R(5), R(0), None, None, None, dv
        ac = kb.q("scalar", "activation", R(3), R(2), AF.Ln, w=[("DV", dv)],
                  i=("AC", 1))
        ac = kb.q("scalar", "activation", R(3), R(3), AF.Exp, scale=-0.5,
                  w=[("AC", ac)], i=("AC", 1))
        dv = kb.q("vector", "scalar_tensor_tensor", R(1), R(2), EPS_RMS, R(1),
                  ALU.mult, ALU.add, i=("DV", 1))
        kb.q("vector", "reciprocal", R(4), R(1))
        dv = kb.q("vector", "tensor_mul", R(4), R(2), R(4), i=("DV", 1))
        ac2 = kb.q("scalar", "activation", R(4), R(4), AF.Sqrt, w=[("DV", dv)],
                   i=("AC", 1))
        kb.q("vector", "tensor_mul", R(5), R(3), R(4), w=[("AC", ac2)])
        s0ic1 = onep = msr = None
        if full:
            kb.q("vector", "reciprocal", R(2), R(4))
            kb.q("vector", "tensor_single_scalar", R(2), R(2), C1, ALU.mult)
            s0ic1 = R(2)
            kb.q("vector", "tensor_single_scalar", R(1), R(3), 1.0, ALU.add)
            onep = R(1)
            kb.q("vector", "tensor_mul", R(4), R(0), R(3))
            msr = R(4)
        kb.q("vector", "tensor_mul", R(0), R(0), R(5))
        dv = kb.q("vector", "engine_nop", i=("DV", 1))
        return R(5), R(0), s0ic1, onep, msr, dv

    def emit_stats_xq(ns, cast_src, dst_tile, full_tt=None, dst_w=()):
        xb16 = ns["xbf16"]
        if "prev" in ns and ns["prev"] is not None:
            wv0 = list(ns["prev"])
        else:
            wv0 = [("DV", kb.cnt["DV"]), ("PE", kb.cnt["PE"])]
        lc = kb.dma("gpsimd", xb16[:, :].rearrange("p (c t) -> p c t", c=KCH),
                    cast_src.rearrange("(c p) t -> p c t", p=128), sem="LC",
                    w=wv0)
        pe_prev = None
        for c in range(KCH):
            xc = xb16[:, c * TTK:(c + 1) * TTK]
            sq = ns["sqbf"][:, 0:TTK]
            wv = [("LC", lc)] if c == 0 else []
            if pe_prev:
                wv.append(("PE", pe_prev))
            dv = kb.q("vector", "tensor_mul", sq, xc, xc, w=wv, i=("DV", 1))
            kb.q("tensor", "matmul", ns["ps_s"][:, :], onescol[:, :], xc,
                 w=([("LC", lc), ("DV", wv0[0][1])] if c == 0 else []),
                 start=(c == 0), stop=(c == KCH - 1))
            pe_prev = kb.q("tensor", "matmul", ns["ps_sq"][:, :], onescol[:, :],
                           sq, w=[("DV", dv)], start=(c == 0),
                           stop=(c == KCH - 1), i=("PE", 1))
        kb.q("vector", "_wait_only_", w=[("PE", kb.cnt["PE"])])
        s1, ms1, s0ic1, onep, msr, dvrow = emit_rows_s1(
            ns["ps_s"], ns["ps_sq"], ns["rowa"], full=(full_tt is not None),
            tt=full_tt)
        if full_tt is not None:
            ns["row_sp"].append(kb.dma("gpsimd", rows_sp[0:1,
                full_tt * TTK:(full_tt + 1) * TTK], onep, sem="SPI",
                w=[("DV", dvrow)]))
            ns["row_sp"].append(kb.dma("gpsimd", rows_sp[1:2,
                full_tt * TTK:(full_tt + 1) * TTK], msr, sem="SPI"))
            ns["row_sp"].append(kb.dma("gpsimd", rows_sp[2:3,
                full_tt * TTK:(full_tt + 1) * TTK], s0ic1, sem="SPI"))
        kb.q("tensor", "matmul", ns["ps_b1"][:, :], ones1[:, :], s1,
             w=[("DV", dvrow)], start=True, stop=True)
        pe = kb.q("tensor", "matmul", ns["ps_b2"][:, :], ones1[:, :], ms1,
                  start=True, stop=True, i=("PE", 1))
        kb.q("vector", "tensor_copy", ns["bcs"][:, 0:TTK], ns["ps_b1"][:, :],
             w=[("PE", pe)])
        dv = kb.q("vector", "tensor_copy", ns["bcs"][:, TTK:2 * TTK],
                  ns["ps_b2"][:, :], i=("DV", 1))
        xq_dv = None
        for c in range(KCH):
            xc = xb16[:, c * TTK:(c + 1) * TTK]
            kb.q("vector", "tensor_mul", ns["t1f"][:, :], xc, ns["bcs"][:, 0:TTK])
            xq_dv = kb.q("vector", "tensor_sub", dst_tile(c), ns["t1f"][:, :],
                         ns["bcs"][:, TTK:2 * TTK], i=("DV", 1),
                         w=dst_w if c == 0 else ())
        ns["prev"] = [("DV", xq_dv), ("PE", kb.cnt["PE"])]
        return xq_dv

    def xqf_slab(par):
        return A_big[:, par * KCH * TTK:(par + 1) * KCH * TTK]

    def emit_foreign_k():
        kb.new_phase()
        from contextlib import ExitStack as _ES
        with _ES() as ph:
            S, P = _stack_alloc(nc, ph)
            xbfA = S("xbfA", [128, KCH * TTK], BF16)
            xbfB = S("xbfB", [128, KCH * TTK], BF16)
            xbf16 = (xbfA, xbfB)
            kslab = S("kslab", [128, KCH * TTK], BF16)
            sqbf = S("sqbf", [128, 2 * TTK], BF16)
            rowa = S("rowa", [1, 3072], F32)
            bcs = S("bcs", [128, 2 * TTK], F32)
            t1f = S("t1f", [128, TTK], F32)
            wtmp = S("wtmp", [128, 2 * TTK], F32)
            redt = S("redt", [128, 2], F32)
            redc = S("redc", [128, 2], F32)
            ps_m = (P("ps_m0", [128, TTK], F32), P("ps_m1", [128, TTK], F32))
            ps_s = (P("ps_s0", [1, TTK], F32), P("ps_s1", [1, TTK], F32))
            ps_sq = (P("ps_sq0", [1, TTK], F32), P("ps_sq1", [1, TTK], F32))
            ps_b1 = P("ps_b1", [128, TTK], F32)
            ps_b2 = P("ps_b2", [128, TTK], F32)

            for j in range(KCH):
                wres_spi = kb.dma(
                    "gpsimd", A_big[:, KW0 + j * 2048:KW0 + (j + 1) * 2048],
                    wqkvT[16 + j], sem="SPI")

            def wres_tile(j, kc):
                bcol = KW0 + j * KCH * 128 + kc * 128
                return A_big[:, bcol:bcol + 128]

            NIT = 12
            lc = {}
            xqdv = {}
            statpe = {}
            bigpe = {}
            rowdv = {}
            bcsdv = {}
            sigbcdv = {}
            sigrowdv = {}
            xqsp = {}
            wsp = {}
            bcast_pe = [None]
            evedge = {0: None, 1: None}

            def emit_cast(it):
                f3, tt = divmod(it, NT)
                par = it % 2
                xbf = xbf16[par]
                if it >= 2:
                    wv = [("DV", xqdv[it - 2]), ("PE", statpe[it - 2])]
                else:
                    wv = [("DV", kb.cnt["DV"]), ("PE", kb.cnt["PE"])]
                lc[it] = kb.dma(
                    "gpsimd", xbf[:, :].rearrange("p (c t) -> p c t", c=KCH),
                    x_for[f3, :, tt * TTK:(tt + 1) * TTK].rearrange(
                        "(c p) t -> p c t", p=128),
                    sem="LC" if par == 0 else "LC1", w=wv)

            def stats_stage(it):
                if it - 2 in pend_red:
                    flush_red(it - 2)
                f3, tt = divmod(it, NT)
                par = it % 2
                xbf = xbf16[par]
                LCs = "LC" if par == 0 else "LC1"
                pe_prev = {0: None, 1: None}
                for c in range(KCH):
                    cp = c % 2
                    xc = xbf[:, c * TTK:(c + 1) * TTK]
                    sq = sqbf[:, cp * TTK:(cp + 1) * TTK]
                    wv = [(LCs, lc[it])] if c == 0 else []
                    if c < 2 and it >= 1:
                        wv.append(("PE", statpe[it - 1]))
                    if pe_prev[cp]:
                        wv.append(("PE", pe_prev[cp]))
                    ac = kb.q("scalar", "activation", sq, xc, AF.Square,
                              w=wv, i=("AC", 1))
                    wv = []
                    if c == 0:
                        wv = [(LCs, lc[it])]
                        if it >= 2:
                            wv.append(("AC", sigrowdv[it - 2]))
                    kb.q("tensor", "matmul", ps_s[par][:, :], onescol[:, :],
                         xc, w=wv, start=(c == 0), stop=(c == KCH - 1))
                    pe_prev[cp] = kb.q("tensor", "matmul", ps_sq[par][:, :],
                                       onescol[:, :], sq, w=[("AC", ac)],
                                       start=(c == 0), stop=(c == KCH - 1),
                                       i=("PE", 1))
                statpe[it] = kb.cnt["PE"]
                kb.q("vector", "_wait_only_", w=[("PE", statpe[it])])
                s1, ms1, _s0, _o, _m, dvrow = emit_rows_s1(
                    ps_s[par], ps_sq[par], rowa)
                rowdv[it] = dvrow
                wv = [("DV", dvrow)]
                if it >= 2:
                    wv.append(("DV", sigbcdv[it - 2]))
                kb.q("tensor", "matmul", ps_b1[:, :], ones1[:, :], s1, w=wv,
                     start=True, stop=True)
                pe = kb.q("tensor", "matmul", ps_b2[:, :], ones1[:, :], ms1,
                          start=True, stop=True, i=("PE", 1))
                bcast_pe[0] = pe
                kb.q("vector", "tensor_copy", bcs[:, 0:TTK], ps_b1[:, :],
                     w=[("PE", pe)])
                bcsdv[it] = kb.q("vector", "tensor_copy", bcs[:, TTK:2 * TTK],
                                 ps_b2[:, :], i=("DV", 1))
                xqf = xqf_slab(par)
                dv = None
                for c in range(KCH):
                    xc = xbf[:, c * TTK:(c + 1) * TTK]
                    wv = []
                    if c == 0:
                        wv = [("PO", kb.cnt["PO"])]
                        if it >= 2:
                            wv += [("PE", bigpe[it - 2]),
                                   (f"SX{par}", xqsp[it - 2])]
                    kb.q("vector", "tensor_mul", t1f[:, :], xc, bcs[:, 0:TTK])
                    dv = kb.q("vector", "tensor_sub",
                              xqf[:, c * TTK:(c + 1) * TTK], t1f[:, :],
                              bcs[:, TTK:2 * TTK], w=wv, i=("DV", 1))
                xqdv[it] = dv
                xqsp[it] = kb.dma("gpsimd", xqf_sp[f3, tt, :, :], xqf[:, :],
                                  sem=f"SX{par}", w=[("DV", dv)])

            def process_k(prev):
                if prev - 1 in pend_red:
                    flush_red(prev - 1)
                f3, tt = divmod(prev, NT)
                par = prev % 2
                xqf = xqf_slab(par)
                acj = {}
                skpe = {0: None, 1: None}
                for j in range(KCH):
                    jp = j % 2
                    for kc in range(KCH):
                        wv = []
                        if kc == 0:
                            if j < 2:
                                wv = [("DV", xqdv[prev]), ("SPI", wres_spi)]
                            if j == 0 and prev - 1 in reddv:
                                wv.append(("DV", reddv[prev - 1]))
                            if evedge[jp]:
                                wv.append(("AC", evedge[jp]))
                        pe = kb.q("tensor", "matmul", ps_m[jp][:, :],
                                  wres_tile(j, kc),
                                  xqf[:, kc * TTK:(kc + 1) * TTK], w=wv,
                                  start=(kc == 0), stop=(kc == KCH - 1),
                                  i=("PE", 1) if kc == KCH - 1 else None)
                    wv = [("PE", pe)]
                    if j < 2 and prev >= 1:
                        wv.append(("SPI", wsp[prev - 1]))
                    ac = kb.q("scalar", "copy",
                              kslab[:, j * TTK:(j + 1) * TTK], ps_m[jp][:, :],
                              w=wv, i=("AC", 1))
                    acj[j] = ac
                    evedge[jp] = ac
                    sq = sqbf[:, jp * TTK:(jp + 1) * TTK]
                    wv = []
                    if j < 2:
                        wv.append(("PE", statpe[prev + 1] if prev + 1 in statpe
                                   else statpe[prev]))
                    if skpe[jp]:
                        wv.append(("PE", skpe[jp]))
                    ac2 = kb.q("scalar", "activation", sq,
                               kslab[:, j * TTK:(j + 1) * TTK], AF.Square,
                               w=wv, i=("AC", 1))
                    wv = [("AC", ac2)]
                    if j == 0:
                        wv.append(("DV", rowdv[prev]))
                    skpe[jp] = kb.q("tensor", "matmul", ps_s[par][:, :],
                                    onescol[:, :], sq, w=wv, start=(j == 0),
                                    stop=(j == KCH - 1), i=("PE", 1))
                sk_pe = kb.cnt["PE"]
                R2 = lambda k2: rowa[:, k2 * TTK:(k2 + 1) * TTK]
                ac = kb.q("scalar", "activation", R2(1), ps_s[par][:, :],
                          AF.Ln, scale=1.0 / D, bias=EPS_RMS,
                          w=[("PE", sk_pe), ("PE", bcast_pe[0])], i=("AC", 1))
                sigrowdv[prev] = ac
                ac = kb.q("scalar", "activation", R2(1), R2(1), AF.Exp,
                          scale=-0.5, w=[("AC", ac)], i=("AC", 1))
                wv = [("AC", ac), ("DV", bcsdv.get(prev + 1, bcsdv[prev]))]
                pe = kb.q("tensor", "matmul", ps_b1[:, :], ones1[:, :], R2(1),
                          w=wv, start=True, stop=True, i=("PE", 1))
                sigbcdv[prev] = kb.q("vector", "tensor_copy", t1f[:, :],
                                     ps_b1[:, :], w=[("PE", pe)], i=("DV", 1))
                ac_p = None
                acw = {}
                for j in range(KCH):
                    jp2 = j % 2
                    ks = kslab[:, j * TTK:(j + 1) * TTK]
                    wt = wtmp[:, jp2 * TTK:(jp2 + 1) * TTK]
                    wv = [("DV", sigbcdv[prev])]
                    if j >= 2:
                        wv.append(("AC", acw[j - 2]))
                    dvm = kb.q("vector", "tensor_mul", wt, ks,
                               t1f[:, :], w=wv, i=("DV", 1))
                    ac_p = kb.q("scalar", "activation", ks, wt, AF.Exp,
                                w=[("DV", dvm)], i=("AC", 1))
                    acw[j] = ac_p
                wsp[prev] = kb.dma("gpsimd", wf_sp[f3, tt, :, :], kslab[:, :],
                                   sem="SPI", w=[("AC", ac_p)])
                pend_red[prev] = (f3, acw)
                bigpe[prev] = kb.cnt["PE"]

            def flush_red(prev):
                f3, acw = pend_red.pop(prev)
                po_mac = {}
                dv = None
                for j in range(KCH):
                    ks = kslab[:, j * TTK:(j + 1) * TTK]
                    rj = redc[:, j % 2:j % 2 + 1]
                    wv2 = [("AC", acw[j])]
                    if j >= 2:
                        wv2.append(("PO", po_mac[j - 2]))
                    elif last_pomac[0]:
                        wv2.append(("PO", last_pomac[0]))
                    dv = kb.q("vector", "reduce_sum", rj, ks, axis=AX.X,
                              w=wv2, i=("DV", 1))
                    tj = redt[:, j % 2:j % 2 + 1]
                    kb.q("gpsimd", "tensor_mul", tj, rj,
                         cmsk[:, f3:f3 + 1], w=[("DV", dv)])
                    po_mac[j] = kb.q("gpsimd", "tensor_add",
                                     carryW[:, j:j + 1], carryW[:, j:j + 1],
                                     tj)
                last_pomac[0] = po_mac[KCH - 1]
                reddv[prev] = dv

            pend_red = {}
            reddv = {}
            last_pomac = [None]
            emit_cast(0)
            emit_cast(1)
            for it in range(12):
                stats_stage(it)
                if it + 2 < 12:
                    emit_cast(it + 2)
                if it >= 1:
                    process_k(it - 1)
            process_k(11)
            flush_red(11)
            kb.run_phase()

    def emit_foreign_v():
        kb.new_phase()
        from contextlib import ExitStack as _ES
        with _ES() as ph:
            S, P = _stack_alloc(nc, ph)
            kslabA = S("kslabA", [128, KCH * TTK], BF16)
            kslabB = S("kslabB", [128, KCH * TTK], BF16)
            kslab2 = (kslabA, kslabB)
            ldbuf = S("ldbuf", [128, 1024], F32)
            vtmp = S("vtmp", [128, TTK], BF16)
            t1f = S("t1f", [128, TTK], F32)
            redc = S("redc", [128, 1], F32)
            ps_m = (P("ps_m0", [128, TTK], F32), P("ps_m1", [128, TTK], F32))

            cprev = None
            for j in range(KCH):
                for hh in range(2):
                    ldv = kb.dma("sync", ldbuf[:, :],
                                 wqkvT[32 + j][:, hh * 8:(hh + 1) * 8, :],
                                 sem="LW0", w=[("AC", cprev)] if cprev else ())
                    cprev = kb.q("scalar", "copy",
                                 A_big[:, KW0 + j * 2048 + hh * 1024:
                                       KW0 + j * 2048 + (hh + 1) * 1024],
                                 ldbuf[:, :], w=[("LW0", ldv)], i=("AC", 1))
            wres_ld = cprev

            def wres_tile(j, kc):
                bcol = KW0 + j * KCH * 128 + kc * 128
                return A_big[:, bcol:bcol + 128]

            ld = {}
            bigpe = {}
            wvdv = {}
            evedge = {0: None, 1: None}
            for it in range(12):
                f3, tt = divmod(it, NT)
                par = it % 2
                wv = [("DV", kb.cnt["DV"])]
                if it >= 2:
                    wv.append(("PE", bigpe[it - 2]))
                kb.dma("sync", xqf_slab(par)[:, :], xqf_sp[f3, tt, :, :],
                       sem=f"LP{par}", w=wv)
                ld[it] = kb.dma("sync", kslab2[par][:, :],
                                wf_sp[f3, tt, :, :], sem=f"LP{par}")
                if it >= 1:
                    prev = it - 1
                    f3p, ttp = divmod(prev, NT)
                    parp = prev % 2
                    _foreign_v_proc(kb, prev, f3p, parp, ld, bigpe, evedge,
                                    wres_tile, wres_ld, xqf_slab, kslab2,
                                    vtmp, t1f, redc, ps_m, cmsk, carryKV)
            _foreign_v_proc(kb, 11, 2, 11 % 2, ld, bigpe, evedge, wres_tile,
                            wres_ld, xqf_slab, kslab2, vtmp, t1f, redc, ps_m,
                            cmsk, carryKV)
            kb.run_phase()

    def _foreign_v_proc(kb, prev, f3, parp, ld, bigpe, evedge, wres_tile,
                        wres_ld, xqf_slab, kslab2, vtmp, t1f, redc, ps_m,
                        cmsk, carryKV):
        xqf = xqf_slab(parp)
        for j in range(KCH):
            jp = j % 2
            for kc in range(KCH):
                wv = []
                if kc == 0:
                    if j == 0:
                        wv = [(f"LP{parp}", ld[prev]), ("AC", wres_ld)]
                    if evedge[jp]:
                        wv.append(("DV", evedge[jp]))
                pe = kb.q("tensor", "matmul", ps_m[jp][:, :],
                          wres_tile(j, kc), xqf[:, kc * TTK:(kc + 1) * TTK],
                          w=wv, start=(kc == 0), stop=(kc == KCH - 1),
                          i=("PE", 1) if kc == KCH - 1 else None)
            dv = kb.q("vector", "tensor_copy", vtmp[:, :], ps_m[jp][:, :],
                      w=[("PE", pe)], i=("DV", 1))
            evedge[jp] = dv
            kb.q("vector", "tensor_mul", t1f[:, :],
                 kslab2[parp][:, j * TTK:(j + 1) * TTK], vtmp[:, :])
            kb.q("vector", "reduce_sum", redc[:, :], t1f[:, :], axis=AX.X)
            kb.q("vector", "scalar_tensor_tensor", carryKV[:, j:j + 1],
                 redc[:, :], cmsk[:, f3:f3 + 1], carryKV[:, j:j + 1],
                 ALU.mult, ALU.add)
        bigpe[prev] = kb.cnt["PE"]

    emit_foreign_k()
    emit_foreign_v()
    if dbg:
        kb.new_phase()
        kb.dma("gpsimd", cw_sp[:, 0:16], carryW[:, :], sem="SPI",
               w=[("DV", kb.cnt["DV"])])
        kb.dma("gpsimd", cw_sp[:, 16:32], carryKV[:, :], sem="SPI")
        kb.run_phase()

    kb.new_phase()
    from contextlib import ExitStack as _ES
    with _ES() as ph:
        S, P = _stack_alloc(nc, ph)
        xbf16 = S("xbf16", [128, KCH * TTK], BF16)
        sqbf = S("sqbf", [128, TTK], BF16)
        bcs = S("bcs", [128, 2 * TTK], F32)
        t1f = S("t1f", [128, TTK], F32)
        rowa = S("rowa", [1, 3072], F32)
        ps_s = P("ps_s", [1, TTK], F32)
        ps_sq = P("ps_sq", [1, TTK], F32)
        ps_b1 = P("ps_b1", [128, TTK], F32)
        ps_b2 = P("ps_b2", [128, TTK], F32)
        ns = dict(xbf16=xbf16, sqbf=sqbf, rowa=rowa, bcs=bcs, t1f=t1f,
                  ps_s=ps_s, ps_sq=ps_sq, ps_b1=ps_b1, ps_b2=ps_b2, row_sp=[])
        for tt in range(NT):
            emit_stats_xq(
                ns, x_own[:, tt * TTK:(tt + 1) * TTK],
                lambda c, tt=tt: xq_tile(c, tt), full_tt=tt)
        kb.run_phase()
        _chk(nc, 3)

    kb.new_phase()
    with _ES() as ph:
        S, P = _stack_alloc(nc, ph)
        xhw = S("xhw", [128, KCH * TTK], F32)
        xhsp = S("xhsp", [128, 8 * TTK], F32)
        s0row = S("s0row", [1, T], F32)
        s0bc = S("s0bc", [128, TTK], F32)
        ps_b1 = P("ps_b1", [128, TTK], F32)
        ps_h0 = P("ps_h0", [128, TTK], F32)
        ps_h1 = P("ps_h1", [128, TTK], F32)
        xhbf = A_big[:, KW0:KW0 + KCH * TTK]
        ps_h = (ps_h0, ps_h1)
        ldr = kb.dma("sync", s0row[:, :], rows_sp[2:3, :])
        xh_ac = None
        xh_sp_done = None
        for tt in range(NT):
            pe = kb.q("tensor", "matmul", ps_b1[:, :], ones1[:, :],
                      s0row[:, tt * TTK:(tt + 1) * TTK],
                      w=[("LD", ldr), ("DV", kb.cnt["DV"])],
                      start=True, stop=True, i=("PE", 1))
            s0bc_dv = kb.q("vector", "tensor_copy", s0bc[:, :], ps_b1[:, :],
                           w=[("PE", pe)], i=("DV", 1))
            evac_dv = {0: None, 1: None}
            for c in range(KCH):
                par = c % 2
                wv = []
                if c == 0:
                    wv = [("DV", s0bc_dv)]
                    if xh_ac:
                        wv.append(("AC", xh_ac))
                if evac_dv[par]:
                    wv.append(("DV", evac_dv[par]))
                pe = kb.q("tensor", "matmul", ps_h[par][:, :],
                          h128su[:, c * 128:(c + 1) * 128], xq_tile(c, tt),
                          w=wv, start=True, stop=True, i=("PE", 1))
                evac_dv[par] = kb.q(
                    "vector", "tensor_mul", xhw[:, c * TTK:(c + 1) * TTK],
                    ps_h[par][:, :], s0bc[:, :],
                    w=[("PE", pe)], i=("DV", 1))
            for s in range(4):
                h = 2 ** s
                width = h * TTK
                r = xhw[:, :].rearrange("p (b two w) -> p b two w", two=2, w=width)
                a_v, b_v = r[:, :, 0, :], r[:, :, 1, :]
                nblk = KCH // (2 * h)
                sp_v = xhsp[:, 0:nblk * width].rearrange("p (b w) -> p b w", w=width)
                kb.q("vector", "tensor_sub", sp_v, a_v, b_v)
                kb.q("vector", "tensor_add", a_v, a_v, b_v)
                dv = kb.q("vector", "tensor_copy", b_v, sp_v, i=("DV", 1))
            wv = [("DV", dv)] + ([("SPI", xh_sp_done)] if xh_sp_done else [])
            xh_ac = kb.q("scalar", "copy", xhbf[:, :], xhw[:, :], w=wv, i=("AC", 1))
            xh_sp_done = kb.dma("gpsimd", xh_sp[tt, :, :], xhbf[:, :], sem="SPI",
                                w=[("AC", xh_ac)])
        kb.run_phase()
        _chk(nc, 4)

    kb.new_phase()
    from contextlib import ExitStack as _ES
    with _ES() as ph:
        S, P = _stack_alloc(nc, ph)
        ldb0 = S("ldb0", [128, 1024], F32)
        ldb1 = S("ldb1", [128, 1024], F32)
        wt0 = S("wt0", [128, KCH * 128], BF16)
        wt1 = S("wt1", [128, KCH * 128], BF16)
        scf = S("scf", [128, 2 * T], F32)
        bca = S("bca", [128, 2 * T], F32)
        rowa = S("rowa", [1, 1024], F32)
        vtmp = S("vtmp", [128, T], BF16)
        vtmp2 = S("vtmp2", [128, T], BF16)
        wvt = S("wvt", [128, T], BF16)
        vslab = (vtmp, vtmp2)
        sqt = wvt
        sqbf = S("sqbf", [128, TTK], BF16)
        qtmp = S("qtmp", [128, 2 * TTK], BF16)
        ps_m0 = P("ps_m0", [128, TTK], F32)
        ps_m1 = P("ps_m1", [128, TTK], F32)
        ps_st0 = P("ps_st0", [1, TTK], F32)
        ps_st1 = P("ps_st1", [1, TTK], F32)
        ps_st2 = P("ps_st2", [1, TTK], F32)
        ps_st3 = P("ps_st3", [1, TTK], F32)
        ps_b1 = P("ps_b1", [128, TTK], F32)
        ps_b2 = P("ps_b2", [128, TTK], F32)
        ps_m = (ps_m0, ps_m1)
        ps_st = (ps_st0, ps_st1, ps_st2, ps_st3)
        ldb, wt = (ldb0, ldb1), (wt0, wt1)
        conv = {0: None, 1: None}
        mmdone = {0: None, 1: None}
        evac = {0: None, 1: None}

        def stream_oc(oc, idx):
            par = idx % 2
            for hh in range(2):
                ldv = kb.dma("sync", ldb[par][:, :],
                             wqkvT[oc][:, hh * 8:(hh + 1) * 8, :],
                             sem=f"LW{par}",
                             w=[("AC", conv[par])] if conv[par] else ())
                wv = [(f"LW{par}", ldv)]
                if mmdone[par] and hh == 0:
                    wv.append(("PE", mmdone[par]))
                conv[par] = kb.q("scalar", "copy",
                                 wt[par][:, hh * 1024:(hh + 1) * 1024],
                                 ldb[par][:, :], w=wv, i=("AC", 1))
            return par

        def mm_tiles(par, tt, pp, extra_w=()):
            psj = ps_m[pp]
            for kc in range(KCH):
                wv = []
                if kc == 0:
                    wv = list(extra_w)
                    wv.append(("AC", conv[par]))
                    if evac[pp]:
                        wv.extend(evac[pp])
                pe = kb.q("tensor", "matmul", psj[:, :],
                          wt[par][:, kc * 128:(kc + 1) * 128], xq_tile(kc, tt),
                          w=wv, start=(kc == 0), stop=(kc == KCH - 1),
                          i=("PE", 1) if kc == KCH - 1 else None)
            return pe

        def sig_rows(ps_stat, tt, dst, ps_bx, ac_gate=None):
            R = lambda i2: rowa[:, i2 * TTK:(i2 + 1) * TTK]
            ac = kb.q("scalar", "activation", R(1), ps_stat[:, :], AF.Ln,
                      scale=1.0 / D, bias=EPS_RMS,
                      w=[("AC", ac_gate if ac_gate is not None
                          else kb.cnt["AC"]), ("PE", kb.cnt["PE"])],
                      i=("AC", 1))
            ac = kb.q("scalar", "activation", R(1), R(1), AF.Exp, scale=-0.5,
                      w=[("AC", ac)], i=("AC", 1))
            pe = kb.q("tensor", "matmul", ps_bx[:, :], ones1[:, :], R(1),
                      w=[("AC", ac), ("DV", kb.cnt["DV"])], start=True, stop=True,
                      i=("PE", 1))
            kb.q("vector", "tensor_copy", dst[:, tt * TTK:(tt + 1) * TTK],
                 ps_bx[:, :], w=[("PE", pe)])

        for j in range(KCH):
            par = stream_oc(16 + j, j)
            for tt in range(NT):
                pp = tt % 2
                pe = mm_tiles(par, tt, pp)
                d1 = kb.q("vector", "tensor_copy", kwr_tile(j, tt),
                          ps_m[pp][:, :], w=[("PE", pe)], i=("DV", 1))
                a2 = kb.q("scalar", "activation", sqbf[:, 0:TTK], kwr_tile(j, tt),
                          AF.Square, w=[("DV", d1)], i=("AC", 1))
                kb.q("tensor", "matmul", ps_st[tt][:, :], onescol[:, :],
                     sqbf[:, 0:TTK], w=[("AC", a2)], start=(j == 0),
                     stop=(j == KCH - 1),
                     i=("PE", 1) if j == KCH - 1 else None)
                evac[pp] = [("DV", d1), ("AC", a2)]
            mmdone[par] = kb.cnt["PE"]
        kb.q("vector", "_wait_only_", w=[("PE", kb.cnt["PE"])])
        vpar = {}
        ac_pre = kb.cnt["AC"]
        vpar[0] = stream_oc(32 + 0, 0)
        vpar[1] = stream_oc(32 + 1, 1)
        for tt in range(NT):
            sig_rows(ps_st[tt], tt, bca[:, 0:T], ps_b1, ac_gate=ac_pre)
        ac_p = None
        wexp_ac = {}
        for j in range(KCH):
            dv = kb.q("vector", "tensor_mul", scf[:, 0:T], kwr_row(j), bca[:, 0:T],
                      w=[("AC", ac_p)] if ac_p else (), i=("DV", 1))
            ac_p = kb.q("scalar", "activation", kwr_row(j), scf[:, 0:T], AF.Exp,
                        w=[("DV", dv)], i=("AC", 1))
            wexp_ac[j] = ac_p

        def v_chain(j):
            kb.q("vector", "tensor_mul", wvt[:, :], kwr_row(j),
                 vslab[j % 2][:, :], w=[("AC", max(wexp_ac[j], evj[j]))])
            kb.q("vector", "tensor_tensor_scan", bca[:, 0:T], kwr_row(j),
                 kwr_row(j), carryW[:, j:j + 1], ALU.add, ALU.bypass)
            kb.q("vector", "tensor_tensor_scan", scf[:, 0:T], wvt[:, :],
                 wvt[:, :], carryKV[:, j:j + 1], ALU.add, ALU.bypass)
            dv0 = kb.q("vector", "tensor_single_scalar", bca[:, 0:T],
                       bca[:, 0:T], 1e-6, ALU.add, i=("DV", 1))
            ac0 = kb.q("scalar", "activation", scf[:, T:2 * T], bca[:, 0:T],
                       AF.Ln, w=[("DV", dv0)], i=("AC", 1))
            ac0 = kb.q("scalar", "activation", scf[:, T:2 * T],
                       scf[:, T:2 * T], AF.Exp, scale=-1.0, w=[("AC", ac0)],
                       i=("AC", 1))
            return kb.q("vector", "tensor_mul", kwr_row(j), scf[:, 0:T],
                        scf[:, T:2 * T], w=[("AC", ac0)], i=("DV", 1))

        evj = {}
        for j in range(KCH):
            par = vpar[j]
            for tt in range(NT):
                pp = tt % 2
                ej = kb.q("scalar", "copy",
                          vslab[j % 2][:, tt * TTK:(tt + 1) * TTK],
                          ps_m[pp][:, :], w=[("PE", mm_tiles(par, tt, pp))],
                          i=("AC", 1))
                evac[pp] = [("AC", ej)]
                evj[j] = ej
            mmdone[par] = kb.cnt["PE"]
            if j + 2 < KCH:
                vpar[j + 2] = stream_oc(32 + j + 2, j + 2)
            if j == KCH - 1:
                qpar0 = stream_oc(0, 0)
                qpar1 = stream_oc(1, 1)
            if j >= 1:
                v_chain(j - 1)
        dv = v_chain(KCH - 1)

        qsp = {}
        qpar = {}
        qpar[0] = qpar0
        qpar[1] = qpar1
        for j in range(KCH):
            par = qpar[j]
            for tt in range(NT):
                pp = tt % 2
                pe = mm_tiles(par, tt, pp)
                wv = [("PE", pe)]
                prev = qsp.get((j, tt - 2), qsp.get((j - 1, tt + 2)))
                if prev:
                    wv.append((f"SQ{tt % 2}", prev))
                dv = kb.q("vector", "tensor_copy", qtmp[:, pp * TTK:(pp + 1) * TTK],
                          ps_m[pp][:, :], w=wv, i=("DV", 1))
                a2 = kb.q("scalar", "activation", sqbf[:, 0:TTK],
                          qtmp[:, pp * TTK:(pp + 1) * TTK], AF.Square,
                          w=[("DV", dv)], i=("AC", 1))
                kb.q("tensor", "matmul", ps_st[tt][:, :], onescol[:, :],
                     sqbf[:, 0:TTK], w=[("AC", a2)], start=(j == 0),
                     stop=(j == KCH - 1),
                     i=("PE", 1) if j == KCH - 1 else None)
                evac[pp] = [("DV", dv), ("AC", a2)]
                qsp[(j, tt)] = kb.dma("gpsimd", q_sp[j, tt, :, :],
                                      qtmp[:, pp * TTK:(pp + 1) * TTK],
                                      sem=f"SQ{tt % 2}", w=[("DV", dv)])
            mmdone[par] = kb.cnt["PE"]
            if j + 2 < KCH:
                qpar[j + 2] = stream_oc(j + 2, j + 2)
        q_mm_done = kb.cnt["PE"]
        kb.q("vector", "_wait_only_", w=[("PE", q_mm_done)])
        for tt in range(NT):
            sig_rows(ps_st[tt], tt, bca[:, T:2 * T], ps_b2)
        ac_p = None
        ydv = {}
        qld = {}

        def q_reload(j):
            jp = j % 2
            half = wvt if jp else vtmp
            for tt in range(NT):
                wv = []
                if tt == 0:
                    wv = [("SQ0", qsp[(j, NT - 2)]), ("SQ1", qsp[(j, NT - 1)])]
                    if j < 2:
                        wv.append(("DV", kb.cnt["DV"]))
                    else:
                        wv.append(("DV", ydv[j - 2]))
                qld[j] = kb.dma("sync", half[:, tt * TTK:(tt + 1) * TTK],
                                q_sp[j, tt, :, :],
                                sem="LQ" if j % 2 == 0 else "LQ1", w=wv)

        q_reload(0)
        q_reload(1)
        for j in range(KCH):
            jp = j % 2
            half = wvt if jp else vtmp
            wv = [("LQ" if jp == 0 else "LQ1", qld[j]), ("PE", q_mm_done)]
            if ac_p:
                wv.append(("AC", ac_p))
            dv = kb.q("vector", "tensor_mul", scf[:, 0:T], half[:, :],
                      bca[:, T:2 * T], w=wv, i=("DV", 1))
            ydv[j] = dv
            ac_p = kb.q("scalar", "activation", scf[:, T:2 * T], scf[:, 0:T],
                        AF.Sigmoid, w=[("DV", dv)], i=("AC", 1))
            kb.q("vector", "tensor_mul", y_row(j), scf[:, T:2 * T], kwr_row(j),
                 w=[("AC", ac_p)])
            if j + 2 < KCH:
                q_reload(j + 2)
        kb.run_phase()
        _chk(nc, 5)

    def emit_pair_phase(wsrc, dst_sp, scaled, pre=None, side_gen_f=None):
        kb.new_phase()
        if pre:
            pre()
        from contextlib import ExitStack as _ES
        with _ES() as ph:
            S, P = _stack_alloc(nc, ph)
            ldb0 = S("ldb0", [128, KCH * 128], F32)
            ldb1 = S("ldb1", [128, KCH * 128], F32)
            wt0 = S("wt0", [128, KCH * 128], BF16)
            wt1 = S("wt1", [128, KCH * 128], BF16)
            ldb = (ldb0, ldb1)
            sgf = S("sgf", [128, 2 * TTK], F32)
            usb = S("usb", [128, 2 * TTK], BF16)
            midb = S("midb", [128, 2 * TTK], BF16)
            mhb = S("mhb", [128, 2 * TTK], BF16) if scaled else midb
            side_gen = None
            if side_gen_f:
                wdb = S("wdb", [128, 64 * 128], F32)
                wspare = S("wspare", [128, 2048], F32)
                side_gen = side_gen_f(wdb, wspare)
            ps_u0 = P("ps_u0", [128, TTK], F32)
            ps_u1 = P("ps_u1", [128, TTK], F32)
            ps_g0 = P("ps_g0", [128, TTK], F32)
            ps_g1 = P("ps_g1", [128, TTK], F32)
            ps_h0 = P("ps_h0", [128, TTK], F32)
            ps_h1 = P("ps_h1", [128, TTK], F32)
            wt = (wt0, wt1)
            ps_u = (ps_u0, ps_u1)
            ps_g = (ps_g0, ps_g1)
            ps_h = (ps_h0, ps_h1)
            ldg = {0: None, 1: None}
            mmdone = {0: None, 1: None}
            ev_u = {0: None, 1: None}
            ev_g = {0: None, 1: None}
            ev_h = {0: None, 1: None}
            mid_dv = {0: None, 1: None}
            last_msp = {0: None, 1: None}
            spq = {}
            pend_h = []
            convp = {0: None, 1: None}
            for p in range(64):
                for half in range(2):
                    oc = 2 * p + half
                    ldv = kb.dma("sync", ldb[half][:, :], wsrc[oc],
                                 sem=f"LW{half}",
                                 w=[("AC", convp[half])] if convp[half] else ())
                    wv = [(f"LW{half}", ldv)]
                    if mmdone[half]:
                        wv.append(("PE", mmdone[half]))
                    convp[half] = kb.q("scalar", "copy", wt[half][:, :],
                                       ldb[half][:, :], w=wv, i=("AC", 1))
                    ldg[half] = convp[half]
                for tt in range(NT):
                    pp = tt % 2
                    for half in range(2):
                        psx = ps_u[pp] if half == 0 else ps_g[pp]
                        evx = ev_u[pp] if half == 0 else ev_g[pp]
                        for kc in range(KCH):
                            wv = []
                            if kc == 0:
                                wv.append(("AC", ldg[half]))
                                if evx:
                                    wv.append(("DV" if half == 0 else "AC", evx))
                            pe = kb.q("tensor", "matmul", psx[:, :],
                                      wt[half][:, kc * 128:(kc + 1) * 128],
                                      y_row(kc)[:, tt * TTK:(tt + 1) * TTK],
                                      w=wv, start=(kc == 0), stop=(kc == KCH - 1),
                                      i=("PE", 1) if kc == KCH - 1 else None)
                        if half == 0:
                            pe_u = pe
                        else:
                            pe_g = pe
                    if pend_h:
                        mp2, gate = pend_h.pop(0)
                        wv = [("DV", gate)]
                        if ev_h[mp2]:
                            wv.append(("AC", ev_h[mp2]))
                        peh = kb.q("tensor", "matmul", ps_h[mp2][:, :],
                                   h128sb[:, :], midb[:, mp2 * TTK:(mp2 + 1) * TTK],
                                   w=wv, start=True, stop=True, i=("PE", 1))
                        wv2 = [("PE", peh)]
                        if last_msp[mp2]:
                            wv2.append((f"SM{mp2}", last_msp[mp2]))
                        ac = kb.q("scalar", "activation",
                                  mhb[:, mp2 * TTK:(mp2 + 1) * TTK],
                                  ps_h[mp2][:, :], AF.Copy, scale=C2,
                                  w=wv2, i=("AC", 1))
                        ev_h[mp2] = ac
                        pkey = hkeys.pop(0)
                        last_msp[mp2] = spq[pkey[1]] = kb.dma(
                            "gpsimd", dst_sp[pkey[0][0], pkey[0][1], :, :],
                            mhb[:, mp2 * TTK:(mp2 + 1) * TTK],
                            sem=f"SM{mp2}", w=[("AC", ac)])
                    S0 = pp * TTK
                    S1 = (pp + 1) * TTK
                    if scaled:
                        ev_g[pp] = kb.q("scalar", "activation", sgf[:, S0:S1],
                                        ps_g[pp][:, :], AF.Silu,
                                        scale=combg[:, p:p + 1],
                                        w=[("PE", pe_g), ("DV", mid_dv[pp])],
                                        i=("AC", 1))
                        ev_u[pp] = kb.q("vector", "tensor_scalar_mul",
                                        usb[:, S0:S1], ps_u[pp][:, :],
                                        combu[:, p:p + 1], w=[("PE", pe_u)],
                                        i=("DV", 1))
                    else:
                        ev_g[pp] = kb.q("scalar", "activation", sgf[:, S0:S1],
                                        ps_g[pp][:, :], AF.Silu,
                                        w=[("PE", pe_g), ("DV", mid_dv[pp])],
                                        i=("AC", 1))
                        ev_u[pp] = kb.q("vector", "tensor_copy", usb[:, S0:S1],
                                        ps_u[pp][:, :], w=[("PE", pe_u)],
                                        i=("DV", 1))
                    if side_gen is not None:
                        next(side_gen, None)
                    key = (p, tt)
                    wv = [("AC", ev_g[pp]), ("DV", ev_u[pp])]
                    if key in spq:
                        wv.append((f"SM{pp}", spq.pop(key)))
                    dv = kb.q("vector", "tensor_mul", midb[:, S0:S1],
                              usb[:, S0:S1], sgf[:, S0:S1], w=wv, i=("DV", 1))
                    mid_dv[pp] = dv
                    if side_gen is not None:
                        next(side_gen, None)
                        next(side_gen, None)
                    nkey = (p, tt + 2) if tt < 2 else (p + 1, tt - 2)
                    if scaled:
                        pend_h.append((pp, dv))
                        hkeys.append(((p, tt), nkey))
                    else:
                        spq[nkey] = kb.dma("gpsimd", dst_sp[p, tt, :, :],
                                           midb[:, S0:S1],
                                           sem=f"SM{pp}", w=[("DV", dv)])
                mmdone[0] = kb.cnt["PE"]
                mmdone[1] = kb.cnt["PE"]
            if side_gen is not None:
                for _ in side_gen:
                    pass
            while pend_h:
                mp2, gate = pend_h.pop(0)
                wv = [("DV", gate)]
                if ev_h[mp2]:
                    wv.append(("AC", ev_h[mp2]))
                peh = kb.q("tensor", "matmul", ps_h[mp2][:, :], h128sb[:, :],
                           midb[:, mp2 * TTK:(mp2 + 1) * TTK], w=wv,
                           start=True, stop=True, i=("PE", 1))
                wv2 = [("PE", peh)]
                if last_msp[mp2]:
                    wv2.append((f"SM{mp2}", last_msp[mp2]))
                ac = kb.q("scalar", "activation",
                          mhb[:, mp2 * TTK:(mp2 + 1) * TTK], ps_h[mp2][:, :],
                          AF.Copy, scale=C2, w=wv2, i=("AC", 1))
                ev_h[mp2] = ac
                pkey = hkeys.pop(0)
                last_msp[mp2] = spq[pkey[1]] = kb.dma(
                    "gpsimd", dst_sp[pkey[0][0], pkey[0][1], :, :],
                    mhb[:, mp2 * TTK:(mp2 + 1) * TTK],
                    sem=f"SM{mp2}", w=[("AC", ac)])
            kb.run_phase()

    hkeys = []

    def what_gen(wdb, wspare):
        sp_prev = None
        for oc in range(16):
            ldv = kb.dma("sync", wdb[:, :], wdownT[oc],
                         w=[("SPI", sp_prev)] if sp_prev else ())
            wv = [("LD", ldv)]
            dv = None
            for s in range(6):
                h = 2 ** s
                width = h * 128
                nblk = 64 // (2 * h)
                r = wdb[:, :].rearrange("p (b two w) -> p b two w", two=2,
                                        w=width)
                if width <= 2048:
                    g = max(1, 2048 // width)
                    for base in range(0, nblk, g):
                        gg = min(g, nblk - base)
                        a_v = r[:, base:base + gg, 0, :]
                        b_v = r[:, base:base + gg, 1, :]
                        sp_v = wspare[:, 0:gg * width].rearrange(
                            "p (b w) -> p b w", w=width)
                        kb.q("vector", "tensor_sub", sp_v, a_v, b_v, w=wv)
                        wv = ()
                        yield
                        kb.q("vector", "tensor_add", a_v, a_v, b_v)
                        yield
                        dv = kb.q("vector", "tensor_copy", b_v, sp_v,
                                  i=("DV", 1))
                        yield
                else:
                    for w0 in range(0, width, 2048):
                        a_v = r[:, 0, 0, w0:w0 + 2048]
                        b_v = r[:, 0, 1, w0:w0 + 2048]
                        kb.q("vector", "tensor_sub", wspare[:, :], a_v, b_v,
                             w=wv)
                        wv = ()
                        yield
                        kb.q("vector", "tensor_add", a_v, a_v, b_v)
                        yield
                        dv = kb.q("vector", "tensor_copy", b_v, wspare[:, :],
                                  i=("DV", 1))
                        yield
            sp_prev = kb.dma("gpsimd", wdh_sp[oc, :, :], wdb[:, :], sem="SPI",
                             w=[("DV", dv)])
            yield

    emit_pair_phase(wswT, psw_sp, scaled=False, side_gen_f=what_gen)

    def reload_xh():
        for tt in range(NT):
            for cg in range(4):
                dst = A_big[:, XQ0:XQ0 + KCH * T].rearrange(
                    "p (c t) -> p c t", c=KCH)[:, cg * 4:(cg + 1) * 4,
                                               tt * TTK:(tt + 1) * TTK]
                src = xh_sp[tt].rearrange("p (c t) -> p c t", c=KCH)[
                    :, cg * 4:(cg + 1) * 4, :]
                kb.dma("sync", dst, src)
        v = kb.cnt["LD"]
        for e in ("tensor", "vector", "scalar"):
            kb.q(e, "_wait_only_", w=[("LD", v)])

    emit_pair_phase(wugT, mid_sp, scaled=True, pre=reload_xh)

    def emit_big_contract(wsrc, cast, act_sp, writer, bc_builder, post=None):
        kb.new_phase()
        from contextlib import ExitStack as _ES
        with _ES() as ph:
            S, P = _stack_alloc(nc, ph)
            wt0 = S("wt0", [128, 64 * 128], BF16)
            wt1 = S("wt1", [128, 64 * 128], BF16)
            aux = S("aux", [128, 2 * TTK], F32)
            ob0 = S("ob0", [128, TTK], F32)
            ob1 = S("ob1", [128, TTK], F32)
            bcx = S("bcx", [128, T], F32)
            ps_o0 = P("ps_o0", [128, TTK], F32)
            ps_o1 = P("ps_o1", [128, TTK], F32)
            ps_bb = P("ps_bb", [128, TTK], F32)
            wt, ob, ps_o = (wt0, wt1), (ob0, ob1), (ps_o0, ps_o1)
            bc_builder(bcx, ps_bb)
            ldb0 = S("ldbA", [128, 1024], F32)
            ldb1 = S("ldbB", [128, 1024], F32)
            ldbp = (ldb0, ldb1)
            convq = {0: None, 1: None}
            ldg = {0: None, 1: None}
            mmdone = {0: None, 1: None}
            evo = {0: None, 1: None}
            obsp = {0: None, 1: None}
            for half in range(2):
                wv = [("PE", kb.cnt["PE"]), ("DV", kb.cnt["DV"]),
                      ("AC", kb.cnt["AC"])]
                wv += [(sn, kb.cnt[sn]) for sn in kb.SPILL_SEMS]
                for p in range(64):
                    srcv = act_sp[p].rearrange("tt p t -> p tt t")[
                        :, 2 * half:2 * half + 2, :]
                    dstv = A_big[:, p * 1024:(p + 1) * 1024].rearrange(
                        "p (tt t) -> p tt t", tt=2)
                    kb.dma("sync", dstv, srcv, w=wv if p == 0 else ())
                act_ld = kb.cnt["LD"]
                kb.q("tensor", "_wait_only_", w=[("LD", act_ld)])
                for o2 in range(16):
                    par = o2 % 2
                    if cast:
                        for pc in range(8):
                            bb = pc % 2
                            wv = [("AC", convq[bb])] if convq[bb] else []
                            ldv = kb.dma("sync", ldbp[bb][:, :],
                                         wsrc(o2)[:, pc * 8:(pc + 1) * 8, :],
                                         sem=f"LW{bb}", w=wv)
                            wv = [(f"LW{bb}", ldv)]
                            if mmdone[par] and pc < 4:
                                wv.append(("PE", mmdone[par]))
                            convq[bb] = kb.q(
                                "scalar", "copy",
                                wt[par][:, pc * 1024:(pc + 1) * 1024],
                                ldbp[bb][:, :], w=wv, i=("AC", 1))
                        ldg[par] = ("AC", convq[1])
                    else:
                        wv = [("PE", mmdone[par])] if mmdone[par] else []
                        ldv = kb.dma("sync", wt[par][:, :], wsrc(o2),
                                     sem=f"LW{par}", w=wv)
                        ldg[par] = (f"LW{par}", ldv)
                    for t2 in range(2):
                        tt = 2 * half + t2
                        pp = t2
                        for kc in range(64):
                            wv = []
                            if kc == 0:
                                wv.append(ldg[par])
                                if evo[pp]:
                                    wv.append(("DV", evo[pp]))
                            pe = kb.q("tensor", "matmul", ps_o[pp][:, :],
                                      wt[par][:, kc * 128:(kc + 1) * 128],
                                      A_big[:, kc * 1024 + t2 * TTK:
                                            kc * 1024 + (t2 + 1) * TTK],
                                      w=wv, start=(kc == 0), stop=(kc == 63),
                                      i=("PE", 1) if kc == 63 else None)
                        evo[pp] = writer(o2, tt, pp, ps_o[pp], ob[pp], pe, bcx,
                                         aux, obsp)
                    mmdone[par] = kb.cnt["PE"]
            if post:
                post()
            kb.run_phase()

    from contextlib import ExitStack as _ES8
    with _ES8() as ph8:
        msrow = ph8.enter_context(nc.sbuf_tensor("msrow", [1, T], F32))
        first_ld = {}

        def bc_ms(bcx, ps_bb):
            ldr = kb.dma("sync", msrow[:, :], rows_sp[1:2, :])
            for tt in range(NT):
                pe = kb.q("tensor", "matmul", ps_bb[:, :], ones1[:, :],
                          msrow[:, tt * TTK:(tt + 1) * TTK],
                          w=[("LD", ldr), ("DV", kb.cnt["DV"])],
                          start=True, stop=True, i=("PE", 1))
                kb.q("vector", "tensor_copy", bcx[:, tt * TTK:(tt + 1) * TTK],
                     ps_bb[:, :], w=[("PE", pe)], i=("DV", 1))

        def wout_writer(o2, tt, pp, psx, obx, pe, bcx, aux, obsp):
            wv = [("PE", pe)]
            if obsp[pp]:
                wv.append((f"SO{pp}", obsp[pp]))
            dv = kb.q("vector", "tensor_sub", obx[:, :], psx[:, :],
                      bcx[:, tt * TTK:(tt + 1) * TTK], w=wv, i=("DV", 1))
            obsp[pp] = kb.dma("gpsimd", proj_sp[o2, tt, :, :], obx[:, :],
                              sem=f"SO{pp}", w=[("DV", dv)])
            return dv

        emit_big_contract(lambda o2: woutT[o2], True, psw_sp, wout_writer,
                          bc_ms)

    from contextlib import ExitStack as _ES9
    with _ES9() as ph9:
        psrow = ph9.enter_context(nc.sbuf_tensor("psrow", [1, T], F32))
        prj = ph9.enter_context(nc.sbuf_tensor("prj", [128, 2 * TTK], F32))
        xin = ph9.enter_context(nc.sbuf_tensor("xin", [128, 2 * TTK], F32))

        def bc_1ps(bcx, ps_bb):
            ldr = kb.dma("sync", psrow[:, :], rows_sp[0:1, :])
            for tt in range(NT):
                pe = kb.q("tensor", "matmul", ps_bb[:, :], ones1[:, :],
                          psrow[:, tt * TTK:(tt + 1) * TTK],
                          w=[("LD", ldr), ("DV", kb.cnt["DV"])],
                          start=True, stop=True, i=("PE", 1))
                kb.q("vector", "tensor_copy", bcx[:, tt * TTK:(tt + 1) * TTK],
                     ps_bb[:, :], w=[("PE", pe)], i=("DV", 1))

        def down_writer(o2, tt, pp, psx, obx, pe, bcx, aux, obsp):
            wv = [("DV", evprev[pp])] if evprev[pp] else ()
            l1 = kb.dma("sync", prj[:, pp * TTK:(pp + 1) * TTK],
                        proj_sp[o2, tt, :, :], sem=f"LP{pp}", w=wv)
            l2 = kb.dma("sync", xin[:, pp * TTK:(pp + 1) * TTK],
                        x_own[o2 * 128:(o2 + 1) * 128, tt * TTK:(tt + 1) * TTK],
                        sem=f"LP{pp}")
            wv = [("PE", pe), (f"LP{pp}", l2)]
            if obsp[pp]:
                wv.append((f"SO{pp}", obsp[pp]))
            kb.q("vector", "scalar_tensor_tensor", obx[:, :], psx[:, :],
                 combdp[:, o2:o2 + 1], prj[:, pp * TTK:(pp + 1) * TTK],
                 ALU.mult, ALU.add, w=wv)
            kb.q("vector", "tensor_mul", aux[:, pp * TTK:(pp + 1) * TTK],
                 xin[:, pp * TTK:(pp + 1) * TTK],
                 bcx[:, tt * TTK:(tt + 1) * TTK], w=[(f"LP{pp}", l2)])
            dv = kb.q("vector", "tensor_add", obx[:, :], obx[:, :],
                      aux[:, pp * TTK:(pp + 1) * TTK], i=("DV", 1))
            evprev[pp] = dv
            obsp[pp] = kb.dma("gpsimd",
                              out_fm[o2 * 128:(o2 + 1) * 128,
                                     tt * TTK:(tt + 1) * TTK], obx[:, :],
                              sem=f"SO{pp}", w=[("DV", dv)])
            return dv

        evprev = {0: None, 1: None}

        def final_wait():
            kb.q("gpsimd", "_wait_only_",
                 w=[(sn, kb.cnt[sn]) for sn in kb.SPILL_SEMS])

        emit_big_contract(lambda o2: wdh_sp[o2], False, mid_sp, down_writer,
                          bc_1ps, post=final_wait)

    es.close()
    return nc


def _tile_w(wT):
    K, O = wT.shape
    kc, oc = K // 128, O // 128
    t = wT.reshape(kc, 128, oc, 128).transpose(2, 1, 0, 3)
    return np.ascontiguousarray(t)


def _prep_inputs(L0_attn_x, L0_attn_w_qkv, L0_attn_w_swiglu, L0_attn_w_out,
                 L0_ffn_SU_ug, L0_ffn_SU_dp, L0_ffn_Wscale_ug, L0_ffn_Wscale_dp,
                 L0_ffn_W_up, L0_ffn_W_gate, L0_ffn_W_down):
    f = np.ascontiguousarray
    x = np.asarray(L0_attn_x, np.float32)
    xt = np.ascontiguousarray(x.transpose(0, 2, 1))

    wqkvT = _tile_w(f(np.asarray(L0_attn_w_qkv, np.float32).T))
    wsw = np.asarray(L0_attn_w_swiglu, np.float32)
    il = np.stack([wsw[:8192].T.reshape(D, 64, 128),
                   wsw[8192:].T.reshape(D, 64, 128)], axis=2).reshape(D, 16384)
    wswT = _tile_w(f(il))
    wup = np.asarray(L0_ffn_W_up, np.float32)
    wga = np.asarray(L0_ffn_W_gate, np.float32)
    il2 = np.stack([wup.T.reshape(D, 64, 128), wga.T.reshape(D, 64, 128)],
                   axis=2).reshape(D, 16384)
    wugT = _tile_w(f(il2))
    woutT = _tile_w(f(np.asarray(L0_attn_w_out, np.float32).T))
    wdownT = f(np.asarray(L0_ffn_W_down, np.float32).T
               .reshape(64, 128, 16, 128).transpose(2, 1, 0, 3))

    c2v = lambda vv, n: f(np.asarray(vv, np.float32).reshape(n, 128).T)
    shared = dict(
        wqkvT=wqkvT, wswT=wswT, wugT=wugT, woutT=woutT, wdownT=wdownT,
        su_ug2=c2v(L0_ffn_SU_ug, 16), su_dp2=c2v(L0_ffn_SU_dp, 64),
        wscu2=c2v(np.asarray(L0_ffn_Wscale_ug, np.float32)[:8192], 64),
        wscg2=c2v(np.asarray(L0_ffn_Wscale_ug, np.float32)[8192:], 64),
        wscdp2=c2v(L0_ffn_Wscale_dp, 16),
        h128in=_had(128).astype(ml_dtypes.bfloat16))
    in_maps = []
    for c in range(8):
        b, l = divmod(c, 4)
        x_own = f(xt[b][:, l * T:(l + 1) * T])
        x_for = np.zeros((3, D, T), np.float32)
        for j in range(l):
            x_for[j] = xt[b][:, j * T:(j + 1) * T]
        cm = np.zeros((128, 4), np.float32)
        cm[:, :l] = 1.0
        m = dict(shared)
        m.update(x_own=x_own, x_for=f(x_for), cmask=cm)
        in_maps.append(m)
    return in_maps


def kernel(**inputs):
    in_maps = _prep_inputs(**inputs)
    nc = build()
    res = run_bass_kernel_spmd(nc, in_maps, core_ids=list(range(8)))
    out = np.empty((2, 8192, D), np.float32)
    for c in range(8):
        b, l = divmod(c, 4)
        out[b, l * T:(l + 1) * T, :] = res.results[c]["out_fm"].T
    return out


# version 20 (final)
# speedup vs baseline: 1.0140x; 1.0140x over previous
import numpy as np
import ml_dtypes
import concourse.bass as bass
import concourse.mybir as mybir
from concourse.bass_utils import run_bass_kernel_spmd

F32 = mybir.dt.float32
BF16 = mybir.dt.bfloat16
AF = mybir.ActivationFunctionType
ALU = mybir.AluOpType
AX = mybir.AxisListType

D = 2048
T = 2048
NT = 4
TTK = 512
KCH = 16
EPS_RMS = 1.1920929e-07
EPS_LN = 1e-5
C1 = float(1.0 / (64.0 * np.sqrt(2048.0)))
C2 = float(1.0 / (64.0 * np.sqrt(8192.0)))

ENGS = ("sync", "tensor", "vector", "scalar", "gpsimd")

_ALLOC_ID = [0]


def _stack_alloc(nc, ph):
    def S(name, shape, dt):
        _ALLOC_ID[0] += 1
        return ph.enter_context(nc.sbuf_tensor(f"{name}_{_ALLOC_ID[0]}", shape, dt))

    def P(name, shape, dt):
        _ALLOC_ID[0] += 1
        return ph.enter_context(nc.psum_tensor(f"{name}_{_ALLOC_ID[0]}", shape, dt))
    return S, P


def _had(n):
    h = np.array([[1.0]], np.float32)
    while h.shape[0] < n:
        h = np.block([[h, h], [h, -h]])
    return h


class KB:

    def __init__(self, nc):
        self.nc = nc
        self.sems = {}
        self.cnt = {n: 0 for n in ("PH", "LD", "LW0", "LW1", "LQ", "LP0", "LP1", "SPI", "SQ0", "SQ1", "SM0", "SM1", "SO0", "SO1", "LC", "LC1", "SX0", "SX1", "LQ1", "PO", "PE", "DV", "AC")}
        self.streams = None

    AUTO = {"vector": "DV", "scalar": "AC", "gpsimd": "PO"}

    def q(self, eng, meth, *args, w=(), i=None, **kw):
        if eng in self.AUTO and meth not in ("_wait_only_", "dma_start"):
            sn = self.AUTO[eng]
            w = tuple(w) + ((sn, self.cnt[sn]),)
            if i is None:
                i = (sn, 1)
        if i is not None:
            self.cnt[i[0]] += i[1]
        v = self.cnt[i[0]] if i is not None else None
        w = tuple((sn, val) for sn, val in w if val)
        self.streams[eng].append((meth, args, kw, w, i))
        return v

    def dma(self, eng, out, in_, sem="LD", w=()):
        return self.q(eng, "dma_start", w=w, i=(sem, 16), out=out, in_=in_)

    def _replay(self, eng, h):
        for meth, args, kw, w, i in self.streams[eng]:
            for sn, val in w:
                h.wait_ge(self.sems[sn], val)
            if meth == "_wait_only_":
                continue
            r = getattr(h, meth)(*args, **kw)
            if i is not None:
                r.then_inc(self.sems[i[0]], i[1])

    def new_phase(self):
        self.streams = {e: [] for e in ENGS}

    SPILL_SEMS = ("SPI", "SQ0", "SQ1", "SM0", "SM1", "SO0", "SO1", "SX0", "SX1")

    def run_phase(self):
        self.cnt["PH"] += 5
        tgt = self.cnt["PH"]
        spw = tuple((sn, self.cnt[sn]) for sn in self.SPILL_SEMS if self.cnt[sn])
        self.streams["gpsimd"].append(("_wait_only_", (), {}, spw, None))
        for e in ENGS:
            self.streams[e].append(("sem_inc", None, {}, (), None))
            self.streams[e].append(("_wait_only_", (), {}, (("PH", tgt),), None))
        with self.nc.Block() as blk:
            for e, fn in (("sync", blk.sync), ("tensor", blk.tensor),
                          ("vector", blk.vector), ("scalar", blk.scalar),
                          ("gpsimd", blk.gpsimd)):
                def body(h, e=e):
                    for meth, args, kw, w, i in self.streams[e]:
                        for sn, val in w:
                            h.wait_ge(self.sems[sn], val)
                        if meth == "_wait_only_":
                            continue
                        if meth == "sem_inc" and args is None:
                            h.sem_inc(self.sems["PH"], 1)
                            continue
                        r = getattr(h, meth)(*args, **kw)
                        if i is not None:
                            r.then_inc(self.sems[i[0]], i[1])
                fn(body)
        self.streams = None


UPTO = 99
F_ITERS = 99
F_PARTS = 99
F_EVAC = 99


class _StopBuild(Exception):
    pass


def _chk(nc, k):
    if k >= UPTO:
        raise _StopBuild(nc)


def build(dbg=False):
    try:
        return _build(dbg)
    except _StopBuild as e:
        return e.args[0]


def _build(dbg=False):
    nc = bass.Bass("TRN2", target_bir_lowering=False)
    kb = KB(nc)
    skind = "ExternalOutput" if dbg else "Internal"
    ap = lambda t: t.ap()

    x_own = ap(nc.dram_tensor("x_own", [D, T], F32, kind="ExternalInput"))
    x_for = ap(nc.dram_tensor("x_for", [3, D, T], F32, kind="ExternalInput"))
    cmask = ap(nc.dram_tensor("cmask", [128, 4], F32, kind="ExternalInput"))
    wqkvT = ap(nc.dram_tensor("wqkvT", [48, 128, KCH, 128], F32, kind="ExternalInput"))
    wswT = ap(nc.dram_tensor("wswT", [128, 128, KCH, 128], F32, kind="ExternalInput"))
    wugT = ap(nc.dram_tensor("wugT", [128, 128, KCH, 128], F32, kind="ExternalInput"))
    woutT = ap(nc.dram_tensor("woutT", [16, 128, 64, 128], F32, kind="ExternalInput"))
    wdownT = ap(nc.dram_tensor("wdownT", [16, 128, 64, 128], F32, kind="ExternalInput"))
    su_ug2 = ap(nc.dram_tensor("su_ug2", [128, 16], F32, kind="ExternalInput"))
    wscu2 = ap(nc.dram_tensor("wscu2", [128, 64], F32, kind="ExternalInput"))
    wscg2 = ap(nc.dram_tensor("wscg2", [128, 64], F32, kind="ExternalInput"))
    wscdp2 = ap(nc.dram_tensor("wscdp2", [128, 16], F32, kind="ExternalInput"))
    su_dp2 = ap(nc.dram_tensor("su_dp2", [128, 64], F32, kind="ExternalInput"))
    h128in = ap(nc.dram_tensor("h128in", [128, 128], BF16, kind="ExternalInput"))
    out_fm = ap(nc.dram_tensor("out_fm", [D, T], F32, kind="ExternalOutput"))

    xqf_sp = ap(nc.dram_tensor("xqf_sp", [3, NT, 128, KCH * TTK], BF16, kind=skind))
    wf_sp = ap(nc.dram_tensor("wf_sp", [3, NT, 128, KCH * TTK], BF16, kind=skind))
    q_sp = ap(nc.dram_tensor("q_sp", [KCH, NT, 128, TTK], BF16, kind=skind))
    xh_sp = ap(nc.dram_tensor("xh_sp", [NT, 128, KCH * TTK], BF16, kind=skind))
    mid_sp = ap(nc.dram_tensor("mid_sp", [64, NT, 128, TTK], BF16, kind=skind))
    psw_sp = ap(nc.dram_tensor("psw_sp", [64, NT, 128, TTK], BF16, kind=skind))
    wdh_sp = ap(nc.dram_tensor("wdh_sp", [16, 128, 64 * 128], BF16, kind=skind))
    proj_sp = ap(nc.dram_tensor("proj_sp", [16, NT, 128, TTK], F32, kind=skind))
    rows_sp = ap(nc.dram_tensor("rows_sp", [3, T], F32, kind=skind))
    cw_sp = ap(nc.dram_tensor("cw_sp", [128, 32], F32, kind=skind)) if dbg else None

    from contextlib import ExitStack
    es = ExitStack()
    for sname in ("PH", "LD", "LW0", "LW1", "LQ", "LP0", "LP1", "SPI", "SQ0",
                  "SQ1", "SM0", "SM1", "SO0", "SO1", "LC", "LC1", "SX0", "SX1", "LQ1", "PO",
                  "PE", "DV", "AC"):
        kb.sems[sname] = es.enter_context(nc.semaphore(sname))

    A_big = es.enter_context(nc.sbuf_tensor("A_big", [128, 65536], BF16))
    h128su = es.enter_context(nc.sbuf_tensor("h128su", [128, 16 * 128], BF16))
    h128sb = es.enter_context(nc.sbuf_tensor("h128sb", [128, 128], BF16))
    ones1 = es.enter_context(nc.sbuf_tensor("ones1", [1, 128], F32))
    onescol = es.enter_context(nc.sbuf_tensor("onescol", [128, 1], BF16))
    epsc = es.enter_context(nc.sbuf_tensor("epsc", [128, 1], F32))
    eps6 = es.enter_context(nc.sbuf_tensor("eps6", [128, 1], F32))
    epsd = es.enter_context(nc.sbuf_tensor("epsd", [128, 1], F32))
    cmsk = es.enter_context(nc.sbuf_tensor("cmsk", [128, 4], F32))
    combu = es.enter_context(nc.sbuf_tensor("combu", [128, 64], F32))
    combg = es.enter_context(nc.sbuf_tensor("combg", [128, 64], F32))
    combdp = es.enter_context(nc.sbuf_tensor("combdp", [128, 16], F32))
    sudp = es.enter_context(nc.sbuf_tensor("sudp", [128, 64], F32))
    suug = es.enter_context(nc.sbuf_tensor("suug", [128, 16], F32))
    carryW = es.enter_context(nc.sbuf_tensor("carryW", [128, 16], F32))
    carryKV = es.enter_context(nc.sbuf_tensor("carryKV", [128, 16], F32))
    tmpv = es.enter_context(nc.sbuf_tensor("tmpv", [128, 64], F32))

    XQ0, KW0 = 0, 32768

    def xq_tile(c, tt):
        b = XQ0 + c * T + tt * TTK
        return A_big[:, b:b + TTK]

    def kwr_row(j):
        return A_big[:, KW0 + j * T:KW0 + (j + 1) * T]

    def y_row(j):
        return A_big[:, j * T:(j + 1) * T]

    def kwr_tile(j, tt):
        b = KW0 + j * T + tt * TTK
        return A_big[:, b:b + TTK]

    kb.new_phase()
    kb.dma("sync", h128sb[:, :], h128in[:, :])
    kb.dma("sync", cmsk[:, :], cmask[:, :])
    kb.dma("sync", sudp[:, :], su_dp2[:, :])
    kb.dma("sync", suug[:, :], su_ug2[:, :])
    v = kb.dma("sync", tmpv[:, :], wscu2[:, :])
    kb.q("vector", "memset", ones1[:, :], 1.0)
    kb.q("vector", "memset", epsc[:, :], EPS_RMS)
    nc.const_aps.register(F32, EPS_RMS, epsc[:, 0:1])
    kb.q("vector", "memset", eps6[:, :], 1e-6)
    nc.const_aps.register(F32, 1e-6, eps6[:, 0:1])
    kb.q("vector", "memset", epsd[:, :], EPS_LN * EPS_RMS)
    nc.const_aps.register(F32, EPS_LN * EPS_RMS, epsd[:, 0:1])
    kb.q("vector", "memset", onescol[:, :], 1.0)
    kb.q("vector", "memset", carryW[:, :], 0.0)
    kb.q("vector", "memset", carryKV[:, :], 0.0)
    kb.q("vector", "tensor_single_scalar", combu[:, :], tmpv[:, :], 64.0, ALU.mult,
         w=[("LD", v)])
    dv = kb.q("vector", "tensor_mul", combu[:, :], combu[:, :], sudp[:, :], i=("DV", 1))
    v = kb.dma("sync", tmpv[:, :], wscg2[:, :], w=[("DV", dv)])
    dv = kb.q("vector", "tensor_single_scalar", combg[:, :], tmpv[:, :], 64.0, ALU.mult,
              w=[("LD", v)], i=("DV", 1))
    v = kb.dma("sync", tmpv[:, 0:16], wscdp2[:, :], w=[("DV", dv)])
    kb.q("vector", "tensor_single_scalar", combdp[:, :], tmpv[:, 0:16], 64.0, ALU.mult,
         w=[("LD", v)])
    for c in range(KCH):
        kb.q("vector", "tensor_scalar_mul", h128su[:, c * 128:(c + 1) * 128],
             h128sb[:, :], suug[:, c:c + 1])
    kb.run_phase()
    _chk(nc, 0)

    def emit_rows_s1(ps_s, ps_sq, rowa, full=False, tt=None):
        R = lambda k: rowa[:, k * TTK:(k + 1) * TTK]
        kb.q("vector", "tensor_single_scalar", R(0), ps_s[:, :], 1.0 / D, ALU.mult,
             w=[("AC", kb.cnt["AC"]), ("SPI", kb.cnt["SPI"])])
        kb.q("vector", "tensor_single_scalar", R(1), ps_sq[:, :], 1.0 / D, ALU.mult)
        kb.q("vector", "tensor_mul", R(4), R(0), R(0))
        kb.q("vector", "tensor_sub", R(1), R(1), R(4))
        if not full:
            dv = kb.q("vector", "engine_nop", i=("DV", 1))
            ac = kb.q("scalar", "activation", R(5), R(1), AF.Ln,
                      scale=1.0 + EPS_RMS, bias=EPS_LN * EPS_RMS,
                      w=[("DV", dv)], i=("AC", 1))
            ac = kb.q("scalar", "activation", R(5), R(5), AF.Exp, scale=-0.5,
                      w=[("AC", ac)], i=("AC", 1))
            kb.q("vector", "tensor_mul", R(0), R(0), R(5), w=[("AC", ac)])
            dv = kb.q("vector", "engine_nop", i=("DV", 1))
            return R(5), R(0), None, None, None, dv
        dv = kb.q("vector", "tensor_single_scalar", R(2), R(1), EPS_LN, ALU.add,
                  i=("DV", 1))
        ac = kb.q("scalar", "activation", R(3), R(2), AF.Ln, w=[("DV", dv)],
                  i=("AC", 1))
        ac = kb.q("scalar", "activation", R(3), R(3), AF.Exp, scale=-0.5,
                  w=[("AC", ac)], i=("AC", 1))
        dv = kb.q("vector", "scalar_tensor_tensor", R(1), R(2), EPS_RMS, R(1),
                  ALU.mult, ALU.add, i=("DV", 1))
        kb.q("vector", "reciprocal", R(4), R(1))
        dv = kb.q("vector", "tensor_mul", R(4), R(2), R(4), i=("DV", 1))
        ac2 = kb.q("scalar", "activation", R(4), R(4), AF.Sqrt, w=[("DV", dv)],
                   i=("AC", 1))
        kb.q("vector", "tensor_mul", R(5), R(3), R(4), w=[("AC", ac2)])
        s0ic1 = onep = msr = None
        if full:
            kb.q("vector", "reciprocal", R(2), R(4))
            kb.q("vector", "tensor_single_scalar", R(2), R(2), C1, ALU.mult)
            s0ic1 = R(2)
            kb.q("vector", "tensor_single_scalar", R(1), R(3), 1.0, ALU.add)
            onep = R(1)
            kb.q("vector", "tensor_mul", R(4), R(0), R(3))
            msr = R(4)
        kb.q("vector", "tensor_mul", R(0), R(0), R(5))
        dv = kb.q("vector", "engine_nop", i=("DV", 1))
        return R(5), R(0), s0ic1, onep, msr, dv

    def emit_stats_xq(ns, cast_src, dst_tile, full_tt=None, dst_w=()):
        xb16 = ns["xbf16"]
        if "prev" in ns and ns["prev"] is not None:
            wv0 = list(ns["prev"])
        else:
            wv0 = [("DV", kb.cnt["DV"]), ("PE", kb.cnt["PE"])]
        lc = kb.dma("gpsimd", xb16[:, :].rearrange("p (c t) -> p c t", c=KCH),
                    cast_src.rearrange("(c p) t -> p c t", p=128), sem="LC",
                    w=wv0)
        pe_prev = None
        for c in range(KCH):
            xc = xb16[:, c * TTK:(c + 1) * TTK]
            sq = ns["sqbf"][:, 0:TTK]
            wv = [("LC", lc)] if c == 0 else []
            if pe_prev:
                wv.append(("PE", pe_prev))
            dv = kb.q("vector", "tensor_mul", sq, xc, xc, w=wv, i=("DV", 1))
            kb.q("tensor", "matmul", ns["ps_s"][:, :], onescol[:, :], xc,
                 w=([("LC", lc), ("DV", wv0[0][1])] if c == 0 else []),
                 start=(c == 0), stop=(c == KCH - 1))
            pe_prev = kb.q("tensor", "matmul", ns["ps_sq"][:, :], onescol[:, :],
                           sq, w=[("DV", dv)], start=(c == 0),
                           stop=(c == KCH - 1), i=("PE", 1))
        kb.q("vector", "_wait_only_", w=[("PE", kb.cnt["PE"])])
        s1, ms1, s0ic1, onep, msr, dvrow = emit_rows_s1(
            ns["ps_s"], ns["ps_sq"], ns["rowa"], full=(full_tt is not None),
            tt=full_tt)
        if full_tt is not None:
            ns["row_sp"].append(kb.dma("gpsimd", rows_sp[0:1,
                full_tt * TTK:(full_tt + 1) * TTK], onep, sem="SPI",
                w=[("DV", dvrow)]))
            ns["row_sp"].append(kb.dma("gpsimd", rows_sp[1:2,
                full_tt * TTK:(full_tt + 1) * TTK], msr, sem="SPI"))
            ns["row_sp"].append(kb.dma("gpsimd", rows_sp[2:3,
                full_tt * TTK:(full_tt + 1) * TTK], s0ic1, sem="SPI"))
        kb.q("tensor", "matmul", ns["ps_b1"][:, :], ones1[:, :], s1,
             w=[("DV", dvrow)], start=True, stop=True)
        pe = kb.q("tensor", "matmul", ns["ps_b2"][:, :], ones1[:, :], ms1,
                  start=True, stop=True, i=("PE", 1))
        kb.q("vector", "tensor_copy", ns["bcs"][:, 0:TTK], ns["ps_b1"][:, :],
             w=[("PE", pe)])
        dv = kb.q("vector", "tensor_copy", ns["bcs"][:, TTK:2 * TTK],
                  ns["ps_b2"][:, :], i=("DV", 1))
        xq_dv = None
        for c in range(KCH):
            xc = xb16[:, c * TTK:(c + 1) * TTK]
            kb.q("vector", "tensor_mul", ns["t1f"][:, :], xc, ns["bcs"][:, 0:TTK])
            xq_dv = kb.q("vector", "tensor_sub", dst_tile(c), ns["t1f"][:, :],
                         ns["bcs"][:, TTK:2 * TTK], i=("DV", 1),
                         w=dst_w if c == 0 else ())
        ns["prev"] = [("DV", xq_dv), ("PE", kb.cnt["PE"])]
        return xq_dv

    def xqf_slab(par):
        return A_big[:, par * KCH * TTK:(par + 1) * KCH * TTK]

    def emit_foreign_k():
        kb.new_phase()
        from contextlib import ExitStack as _ES
        with _ES() as ph:
            S, P = _stack_alloc(nc, ph)
            xbfA = S("xbfA", [128, KCH * TTK], BF16)
            xbfB = S("xbfB", [128, KCH * TTK], BF16)
            xbf16 = (xbfA, xbfB)
            kslab = S("kslab", [128, KCH * TTK], BF16)
            sqbf = S("sqbf", [128, 2 * TTK], BF16)
            rowa = S("rowa", [1, 3072], F32)
            bcs = S("bcs", [128, 2 * TTK], F32)
            t1f = S("t1f", [128, TTK], F32)
            wtmp = S("wtmp", [128, 2 * TTK], F32)
            redt = S("redt", [128, 2], F32)
            redc = S("redc", [128, 2], F32)
            ps_m = (P("ps_m0", [128, TTK], F32), P("ps_m1", [128, TTK], F32))
            ps_s = (P("ps_s0", [1, TTK], F32), P("ps_s1", [1, TTK], F32))
            ps_sq = (P("ps_sq0", [1, TTK], F32), P("ps_sq1", [1, TTK], F32))
            ps_b1 = P("ps_b1", [128, TTK], F32)
            ps_b2 = P("ps_b2", [128, TTK], F32)

            for j in range(KCH):
                wres_spi = kb.dma(
                    "gpsimd", A_big[:, KW0 + j * 2048:KW0 + (j + 1) * 2048],
                    wqkvT[16 + j], sem="SPI")

            def wres_tile(j, kc):
                bcol = KW0 + j * KCH * 128 + kc * 128
                return A_big[:, bcol:bcol + 128]

            NIT = 12
            lc = {}
            xqdv = {}
            statpe = {}
            bigpe = {}
            rowdv = {}
            bcsdv = {}
            sigbcdv = {}
            sigrowdv = {}
            xqsp = {}
            wsp = {}
            bcast_pe = [None]
            evedge = {0: None, 1: None}

            def emit_cast(it):
                f3, tt = divmod(it, NT)
                par = it % 2
                xbf = xbf16[par]
                if it >= 2:
                    wv = [("DV", xqdv[it - 2]), ("PE", statpe[it - 2])]
                else:
                    wv = [("DV", kb.cnt["DV"]), ("PE", kb.cnt["PE"])]
                lc[it] = kb.dma(
                    "gpsimd", xbf[:, :].rearrange("p (c t) -> p c t", c=KCH),
                    x_for[f3, :, tt * TTK:(tt + 1) * TTK].rearrange(
                        "(c p) t -> p c t", p=128),
                    sem="LC" if par == 0 else "LC1", w=wv)

            def stats_stage(it):
                if it - 2 in pend_red:
                    flush_red(it - 2)
                f3, tt = divmod(it, NT)
                par = it % 2
                xbf = xbf16[par]
                LCs = "LC" if par == 0 else "LC1"
                pe_prev = {0: None, 1: None}
                for c in range(KCH):
                    cp = c % 2
                    xc = xbf[:, c * TTK:(c + 1) * TTK]
                    sq = sqbf[:, cp * TTK:(cp + 1) * TTK]
                    wv = [(LCs, lc[it])] if c == 0 else []
                    if c < 2 and it >= 1:
                        wv.append(("PE", statpe[it - 1]))
                    if pe_prev[cp]:
                        wv.append(("PE", pe_prev[cp]))
                    ac = kb.q("scalar", "activation", sq, xc, AF.Square,
                              w=wv, i=("AC", 1))
                    wv = []
                    if c == 0:
                        wv = [(LCs, lc[it])]
                        if it >= 2:
                            wv.append(("AC", sigrowdv[it - 2]))
                    kb.q("tensor", "matmul", ps_s[par][:, :], onescol[:, :],
                         xc, w=wv, start=(c == 0), stop=(c == KCH - 1))
                    pe_prev[cp] = kb.q("tensor", "matmul", ps_sq[par][:, :],
                                       onescol[:, :], sq, w=[("AC", ac)],
                                       start=(c == 0), stop=(c == KCH - 1),
                                       i=("PE", 1))
                statpe[it] = kb.cnt["PE"]
                kb.q("vector", "_wait_only_", w=[("PE", statpe[it])])
                s1, ms1, _s0, _o, _m, dvrow = emit_rows_s1(
                    ps_s[par], ps_sq[par], rowa)
                rowdv[it] = dvrow
                wv = [("DV", dvrow)]
                if it >= 2:
                    wv.append(("DV", sigbcdv[it - 2]))
                kb.q("tensor", "matmul", ps_b1[:, :], ones1[:, :], s1, w=wv,
                     start=True, stop=True)
                pe = kb.q("tensor", "matmul", ps_b2[:, :], ones1[:, :], ms1,
                          start=True, stop=True, i=("PE", 1))
                bcast_pe[0] = pe
                kb.q("vector", "tensor_copy", bcs[:, 0:TTK], ps_b1[:, :],
                     w=[("PE", pe)])
                bcsdv[it] = kb.q("vector", "tensor_copy", bcs[:, TTK:2 * TTK],
                                 ps_b2[:, :], i=("DV", 1))
                xqf = xqf_slab(par)
                dv = None
                for c in range(KCH):
                    xc = xbf[:, c * TTK:(c + 1) * TTK]
                    wv = []
                    if c == 0:
                        wv = [("PO", kb.cnt["PO"])]
                        if it >= 2:
                            wv += [("PE", bigpe[it - 2]),
                                   (f"SX{par}", xqsp[it - 2])]
                    kb.q("vector", "tensor_mul", t1f[:, :], xc, bcs[:, 0:TTK])
                    dv = kb.q("vector", "tensor_sub",
                              xqf[:, c * TTK:(c + 1) * TTK], t1f[:, :],
                              bcs[:, TTK:2 * TTK], w=wv, i=("DV", 1))
                xqdv[it] = dv
                xqsp[it] = kb.dma("gpsimd", xqf_sp[f3, tt, :, :], xqf[:, :],
                                  sem=f"SX{par}", w=[("DV", dv)])

            def process_k(prev):
                if prev - 1 in pend_red:
                    flush_red(prev - 1)
                f3, tt = divmod(prev, NT)
                par = prev % 2
                xqf = xqf_slab(par)
                acj = {}
                skpe = {0: None, 1: None}
                for j in range(KCH):
                    jp = j % 2
                    for kc in range(KCH):
                        wv = []
                        if kc == 0:
                            if j < 2:
                                wv = [("DV", xqdv[prev]), ("SPI", wres_spi)]
                            if j == 0 and prev - 1 in reddv:
                                wv.append(("DV", reddv[prev - 1]))
                            if evedge[jp]:
                                wv.append(("AC", evedge[jp]))
                        pe = kb.q("tensor", "matmul", ps_m[jp][:, :],
                                  wres_tile(j, kc),
                                  xqf[:, kc * TTK:(kc + 1) * TTK], w=wv,
                                  start=(kc == 0), stop=(kc == KCH - 1),
                                  i=("PE", 1) if kc == KCH - 1 else None)
                    wv = [("PE", pe)]
                    if j < 2 and prev >= 1:
                        wv.append(("SPI", wsp[prev - 1]))
                    ac = kb.q("scalar", "copy",
                              kslab[:, j * TTK:(j + 1) * TTK], ps_m[jp][:, :],
                              w=wv, i=("AC", 1))
                    acj[j] = ac
                    evedge[jp] = ac
                    sq = sqbf[:, jp * TTK:(jp + 1) * TTK]
                    wv = []
                    if j < 2:
                        wv.append(("PE", statpe[prev + 1] if prev + 1 in statpe
                                   else statpe[prev]))
                    if skpe[jp]:
                        wv.append(("PE", skpe[jp]))
                    ac2 = kb.q("scalar", "activation", sq,
                               kslab[:, j * TTK:(j + 1) * TTK], AF.Square,
                               w=wv, i=("AC", 1))
                    wv = [("AC", ac2)]
                    if j == 0:
                        wv.append(("DV", rowdv[prev]))
                    skpe[jp] = kb.q("tensor", "matmul", ps_s[par][:, :],
                                    onescol[:, :], sq, w=wv, start=(j == 0),
                                    stop=(j == KCH - 1), i=("PE", 1))
                sk_pe = kb.cnt["PE"]
                R2 = lambda k2: rowa[:, k2 * TTK:(k2 + 1) * TTK]
                ac = kb.q("scalar", "activation", R2(1), ps_s[par][:, :],
                          AF.Ln, scale=1.0 / D, bias=EPS_RMS,
                          w=[("PE", sk_pe), ("PE", bcast_pe[0])], i=("AC", 1))
                sigrowdv[prev] = ac
                ac = kb.q("scalar", "activation", R2(1), R2(1), AF.Exp,
                          scale=-0.5, w=[("AC", ac)], i=("AC", 1))
                wv = [("AC", ac), ("DV", bcsdv.get(prev + 1, bcsdv[prev]))]
                pe = kb.q("tensor", "matmul", ps_b1[:, :], ones1[:, :], R2(1),
                          w=wv, start=True, stop=True, i=("PE", 1))
                sigbcdv[prev] = kb.q("vector", "tensor_copy", t1f[:, :],
                                     ps_b1[:, :], w=[("PE", pe)], i=("DV", 1))
                ac_p = None
                acw = {}
                for j in range(KCH):
                    jp2 = j % 2
                    ks = kslab[:, j * TTK:(j + 1) * TTK]
                    wt = wtmp[:, jp2 * TTK:(jp2 + 1) * TTK]
                    wv = [("DV", sigbcdv[prev])]
                    if j >= 2:
                        wv.append(("AC", acw[j - 2]))
                    dvm = kb.q("vector", "tensor_mul", wt, ks,
                               t1f[:, :], w=wv, i=("DV", 1))
                    ac_p = kb.q("scalar", "activation", ks, wt, AF.Exp,
                                w=[("DV", dvm)], i=("AC", 1))
                    acw[j] = ac_p
                wsp[prev] = kb.dma("gpsimd", wf_sp[f3, tt, :, :], kslab[:, :],
                                   sem="SPI", w=[("AC", ac_p)])
                pend_red[prev] = (f3, acw)
                bigpe[prev] = kb.cnt["PE"]

            def flush_red(prev):
                f3, acw = pend_red.pop(prev)
                po_mac = {}
                dv = None
                for j in range(KCH):
                    ks = kslab[:, j * TTK:(j + 1) * TTK]
                    rj = redc[:, j % 2:j % 2 + 1]
                    wv2 = [("AC", acw[j])]
                    if j >= 2:
                        wv2.append(("PO", po_mac[j - 2]))
                    elif last_pomac[0]:
                        wv2.append(("PO", last_pomac[0]))
                    dv = kb.q("vector", "reduce_sum", rj, ks, axis=AX.X,
                              w=wv2, i=("DV", 1))
                    tj = redt[:, j % 2:j % 2 + 1]
                    kb.q("gpsimd", "tensor_mul", tj, rj,
                         cmsk[:, f3:f3 + 1], w=[("DV", dv)])
                    po_mac[j] = kb.q("gpsimd", "tensor_add",
                                     carryW[:, j:j + 1], carryW[:, j:j + 1],
                                     tj)
                last_pomac[0] = po_mac[KCH - 1]
                reddv[prev] = dv

            pend_red = {}
            reddv = {}
            last_pomac = [None]
            emit_cast(0)
            emit_cast(1)
            for it in range(12):
                stats_stage(it)
                if it + 2 < 12:
                    emit_cast(it + 2)
                if it >= 1:
                    process_k(it - 1)
            process_k(11)
            flush_red(11)
            kb.run_phase()

    def emit_foreign_v():
        kb.new_phase()
        from contextlib import ExitStack as _ES
        with _ES() as ph:
            S, P = _stack_alloc(nc, ph)
            kslabA = S("kslabA", [128, KCH * TTK], BF16)
            kslabB = S("kslabB", [128, KCH * TTK], BF16)
            kslab2 = (kslabA, kslabB)
            ldbuf = S("ldbuf", [128, 1024], F32)
            vtmp = S("vtmp", [128, TTK], BF16)
            t1f = S("t1f", [128, TTK], F32)
            redc = S("redc", [128, 1], F32)
            ps_m = (P("ps_m0", [128, TTK], F32), P("ps_m1", [128, TTK], F32))

            cprev = None
            for j in range(KCH):
                for hh in range(2):
                    ldv = kb.dma("sync", ldbuf[:, :],
                                 wqkvT[32 + j][:, hh * 8:(hh + 1) * 8, :],
                                 sem="LW0", w=[("AC", cprev)] if cprev else ())
                    cprev = kb.q("scalar", "copy",
                                 A_big[:, KW0 + j * 2048 + hh * 1024:
                                       KW0 + j * 2048 + (hh + 1) * 1024],
                                 ldbuf[:, :], w=[("LW0", ldv)], i=("AC", 1))
            wres_ld = cprev

            def wres_tile(j, kc):
                bcol = KW0 + j * KCH * 128 + kc * 128
                return A_big[:, bcol:bcol + 128]

            ld = {}
            bigpe = {}
            wvdv = {}
            evedge = {0: None, 1: None}
            for it in range(12):
                f3, tt = divmod(it, NT)
                par = it % 2
                wv = [("DV", kb.cnt["DV"])]
                if it >= 2:
                    wv.append(("PE", bigpe[it - 2]))
                kb.dma("sync", xqf_slab(par)[:, :], xqf_sp[f3, tt, :, :],
                       sem=f"LP{par}", w=wv)
                ld[it] = kb.dma("sync", kslab2[par][:, :],
                                wf_sp[f3, tt, :, :], sem=f"LP{par}")
                if it >= 1:
                    prev = it - 1
                    f3p, ttp = divmod(prev, NT)
                    parp = prev % 2
                    _foreign_v_proc(kb, prev, f3p, parp, ld, bigpe, evedge,
                                    wres_tile, wres_ld, xqf_slab, kslab2,
                                    vtmp, t1f, redc, ps_m, cmsk, carryKV)
            _foreign_v_proc(kb, 11, 2, 11 % 2, ld, bigpe, evedge, wres_tile,
                            wres_ld, xqf_slab, kslab2, vtmp, t1f, redc, ps_m,
                            cmsk, carryKV)
            kb.run_phase()

    def _foreign_v_proc(kb, prev, f3, parp, ld, bigpe, evedge, wres_tile,
                        wres_ld, xqf_slab, kslab2, vtmp, t1f, redc, ps_m,
                        cmsk, carryKV):
        xqf = xqf_slab(parp)
        for j in range(KCH):
            jp = j % 2
            for kc in range(KCH):
                wv = []
                if kc == 0:
                    if j == 0:
                        wv = [(f"LP{parp}", ld[prev]), ("AC", wres_ld)]
                    if evedge[jp]:
                        wv.append(("DV", evedge[jp]))
                pe = kb.q("tensor", "matmul", ps_m[jp][:, :],
                          wres_tile(j, kc), xqf[:, kc * TTK:(kc + 1) * TTK],
                          w=wv, start=(kc == 0), stop=(kc == KCH - 1),
                          i=("PE", 1) if kc == KCH - 1 else None)
            dv = kb.q("vector", "tensor_copy", vtmp[:, :], ps_m[jp][:, :],
                      w=[("PE", pe)], i=("DV", 1))
            evedge[jp] = dv
            kb.q("vector", "tensor_mul", t1f[:, :],
                 kslab2[parp][:, j * TTK:(j + 1) * TTK], vtmp[:, :])
            kb.q("vector", "reduce_sum", redc[:, :], t1f[:, :], axis=AX.X)
            kb.q("vector", "scalar_tensor_tensor", carryKV[:, j:j + 1],
                 redc[:, :], cmsk[:, f3:f3 + 1], carryKV[:, j:j + 1],
                 ALU.mult, ALU.add)
        bigpe[prev] = kb.cnt["PE"]

    emit_foreign_k()
    emit_foreign_v()
    if dbg:
        kb.new_phase()
        kb.dma("gpsimd", cw_sp[:, 0:16], carryW[:, :], sem="SPI",
               w=[("DV", kb.cnt["DV"])])
        kb.dma("gpsimd", cw_sp[:, 16:32], carryKV[:, :], sem="SPI")
        kb.run_phase()

    kb.new_phase()
    from contextlib import ExitStack as _ES
    with _ES() as ph:
        S, P = _stack_alloc(nc, ph)
        xbf16 = S("xbf16", [128, KCH * TTK], BF16)
        sqbf = S("sqbf", [128, TTK], BF16)
        bcs = S("bcs", [128, 2 * TTK], F32)
        t1f = S("t1f", [128, TTK], F32)
        rowa = S("rowa", [1, 3072], F32)
        ps_s = P("ps_s", [1, TTK], F32)
        ps_sq = P("ps_sq", [1, TTK], F32)
        ps_b1 = P("ps_b1", [128, TTK], F32)
        ps_b2 = P("ps_b2", [128, TTK], F32)
        ns = dict(xbf16=xbf16, sqbf=sqbf, rowa=rowa, bcs=bcs, t1f=t1f,
                  ps_s=ps_s, ps_sq=ps_sq, ps_b1=ps_b1, ps_b2=ps_b2, row_sp=[])
        for tt in range(NT):
            emit_stats_xq(
                ns, x_own[:, tt * TTK:(tt + 1) * TTK],
                lambda c, tt=tt: xq_tile(c, tt), full_tt=tt)
        kb.run_phase()
        _chk(nc, 3)

    kb.new_phase()
    with _ES() as ph:
        S, P = _stack_alloc(nc, ph)
        xhw = S("xhw", [128, KCH * TTK], F32)
        xhsp = S("xhsp", [128, 8 * TTK], F32)
        s0row = S("s0row", [1, T], F32)
        s0bc = S("s0bc", [128, TTK], F32)
        ps_b1 = P("ps_b1", [128, TTK], F32)
        ps_h0 = P("ps_h0", [128, TTK], F32)
        ps_h1 = P("ps_h1", [128, TTK], F32)
        xhbf = A_big[:, KW0:KW0 + KCH * TTK]
        ps_h = (ps_h0, ps_h1)
        ldr = kb.dma("sync", s0row[:, :], rows_sp[2:3, :])
        xh_ac = None
        xh_sp_done = None
        for tt in range(NT):
            pe = kb.q("tensor", "matmul", ps_b1[:, :], ones1[:, :],
                      s0row[:, tt * TTK:(tt + 1) * TTK],
                      w=[("LD", ldr), ("DV", kb.cnt["DV"])],
                      start=True, stop=True, i=("PE", 1))
            s0bc_dv = kb.q("vector", "tensor_copy", s0bc[:, :], ps_b1[:, :],
                           w=[("PE", pe)], i=("DV", 1))
            evac_dv = {0: None, 1: None}
            for c in range(KCH):
                par = c % 2
                wv = []
                if c == 0:
                    wv = [("DV", s0bc_dv)]
                    if xh_ac:
                        wv.append(("AC", xh_ac))
                if evac_dv[par]:
                    wv.append(("DV", evac_dv[par]))
                pe = kb.q("tensor", "matmul", ps_h[par][:, :],
                          h128su[:, c * 128:(c + 1) * 128], xq_tile(c, tt),
                          w=wv, start=True, stop=True, i=("PE", 1))
                evac_dv[par] = kb.q(
                    "vector", "tensor_mul", xhw[:, c * TTK:(c + 1) * TTK],
                    ps_h[par][:, :], s0bc[:, :],
                    w=[("PE", pe)], i=("DV", 1))
            for s in range(4):
                h = 2 ** s
                width = h * TTK
                r = xhw[:, :].rearrange("p (b two w) -> p b two w", two=2, w=width)
                a_v, b_v = r[:, :, 0, :], r[:, :, 1, :]
                nblk = KCH // (2 * h)
                sp_v = xhsp[:, 0:nblk * width].rearrange("p (b w) -> p b w", w=width)
                kb.q("vector", "tensor_sub", sp_v, a_v, b_v)
                kb.q("vector", "tensor_add", a_v, a_v, b_v)
                dv = kb.q("vector", "tensor_copy", b_v, sp_v, i=("DV", 1))
            wv = [("DV", dv)] + ([("SPI", xh_sp_done)] if xh_sp_done else [])
            xh_ac = kb.q("scalar", "copy", xhbf[:, :], xhw[:, :], w=wv, i=("AC", 1))
            xh_sp_done = kb.dma("gpsimd", xh_sp[tt, :, :], xhbf[:, :], sem="SPI",
                                w=[("AC", xh_ac)])
        kb.run_phase()
        _chk(nc, 4)

    kb.new_phase()
    from contextlib import ExitStack as _ES
    with _ES() as ph:
        S, P = _stack_alloc(nc, ph)
        ldb0 = S("ldb0", [128, 1024], F32)
        ldb1 = S("ldb1", [128, 1024], F32)
        wt0 = S("wt0", [128, KCH * 128], BF16)
        wt1 = S("wt1", [128, KCH * 128], BF16)
        scf = S("scf", [128, 2 * T], F32)
        bca = S("bca", [128, 2 * T], F32)
        rowa = S("rowa", [1, 1024], F32)
        vtmp = S("vtmp", [128, T], BF16)
        vtmp2 = S("vtmp2", [128, T], BF16)
        wvt = S("wvt", [128, T], BF16)
        vslab = (vtmp, vtmp2)
        sqt = wvt
        sqbf = S("sqbf", [128, TTK], BF16)
        qtmp = S("qtmp", [128, 2 * TTK], BF16)
        ps_m0 = P("ps_m0", [128, TTK], F32)
        ps_m1 = P("ps_m1", [128, TTK], F32)
        ps_st0 = P("ps_st0", [1, TTK], F32)
        ps_st1 = P("ps_st1", [1, TTK], F32)
        ps_st2 = P("ps_st2", [1, TTK], F32)
        ps_st3 = P("ps_st3", [1, TTK], F32)
        ps_b1 = P("ps_b1", [128, TTK], F32)
        ps_b2 = P("ps_b2", [128, TTK], F32)
        ps_m = (ps_m0, ps_m1)
        ps_st = (ps_st0, ps_st1, ps_st2, ps_st3)
        ldb, wt = (ldb0, ldb1), (wt0, wt1)
        conv = {0: None, 1: None}
        mmdone = {0: None, 1: None}
        evac = {0: None, 1: None}

        def stream_oc(oc, idx):
            par = idx % 2
            for hh in range(2):
                ldv = kb.dma("sync", ldb[par][:, :],
                             wqkvT[oc][:, hh * 8:(hh + 1) * 8, :],
                             sem=f"LW{par}",
                             w=[("AC", conv[par])] if conv[par] else ())
                wv = [(f"LW{par}", ldv)]
                if mmdone[par] and hh == 0:
                    wv.append(("PE", mmdone[par]))
                conv[par] = kb.q("scalar", "copy",
                                 wt[par][:, hh * 1024:(hh + 1) * 1024],
                                 ldb[par][:, :], w=wv, i=("AC", 1))
            return par

        def mm_tiles(par, tt, pp, extra_w=()):
            psj = ps_m[pp]
            for kc in range(KCH):
                wv = []
                if kc == 0:
                    wv = list(extra_w)
                    wv.append(("AC", conv[par]))
                    if evac[pp]:
                        wv.extend(evac[pp])
                pe = kb.q("tensor", "matmul", psj[:, :],
                          wt[par][:, kc * 128:(kc + 1) * 128], xq_tile(kc, tt),
                          w=wv, start=(kc == 0), stop=(kc == KCH - 1),
                          i=("PE", 1) if kc == KCH - 1 else None)
            return pe

        def sig_rows(ps_stat, tt, dst, ps_bx, ac_gate=None):
            R = lambda i2: rowa[:, i2 * TTK:(i2 + 1) * TTK]
            ac = kb.q("scalar", "activation", R(1), ps_stat[:, :], AF.Ln,
                      scale=1.0 / D, bias=EPS_RMS,
                      w=[("AC", ac_gate if ac_gate is not None
                          else kb.cnt["AC"]), ("PE", kb.cnt["PE"])],
                      i=("AC", 1))
            ac = kb.q("scalar", "activation", R(1), R(1), AF.Exp, scale=-0.5,
                      w=[("AC", ac)], i=("AC", 1))
            pe = kb.q("tensor", "matmul", ps_bx[:, :], ones1[:, :], R(1),
                      w=[("AC", ac), ("DV", kb.cnt["DV"])], start=True, stop=True,
                      i=("PE", 1))
            kb.q("vector", "tensor_copy", dst[:, tt * TTK:(tt + 1) * TTK],
                 ps_bx[:, :], w=[("PE", pe)])

        for j in range(KCH):
            par = stream_oc(16 + j, j)
            for tt in range(NT):
                pp = tt % 2
                pe = mm_tiles(par, tt, pp)
                d1 = kb.q("vector", "tensor_copy", kwr_tile(j, tt),
                          ps_m[pp][:, :], w=[("PE", pe)], i=("DV", 1))
                a2 = kb.q("scalar", "activation", sqbf[:, 0:TTK], kwr_tile(j, tt),
                          AF.Square, w=[("DV", d1)], i=("AC", 1))
                kb.q("tensor", "matmul", ps_st[tt][:, :], onescol[:, :],
                     sqbf[:, 0:TTK], w=[("AC", a2)], start=(j == 0),
                     stop=(j == KCH - 1),
                     i=("PE", 1) if j == KCH - 1 else None)
                evac[pp] = [("DV", d1), ("AC", a2)]
            mmdone[par] = kb.cnt["PE"]
        kb.q("vector", "_wait_only_", w=[("PE", kb.cnt["PE"])])
        vpar = {}
        ac_pre = kb.cnt["AC"]
        vpar[0] = stream_oc(32 + 0, 0)
        vpar[1] = stream_oc(32 + 1, 1)
        for tt in range(NT):
            sig_rows(ps_st[tt], tt, bca[:, 0:T], ps_b1, ac_gate=ac_pre)
        ac_p = None
        wexp_ac = {}
        for j in range(KCH):
            dv = kb.q("vector", "tensor_mul", scf[:, 0:T], kwr_row(j), bca[:, 0:T],
                      w=[("AC", ac_p)] if ac_p else (), i=("DV", 1))
            ac_p = kb.q("scalar", "activation", kwr_row(j), scf[:, 0:T], AF.Exp,
                        w=[("DV", dv)], i=("AC", 1))
            wexp_ac[j] = ac_p

        def v_chain(j):
            kb.q("vector", "tensor_mul", wvt[:, :], kwr_row(j),
                 vslab[j % 2][:, :], w=[("AC", max(wexp_ac[j], evj[j]))])
            kb.q("vector", "tensor_tensor_scan", bca[:, 0:T], kwr_row(j),
                 kwr_row(j), carryW[:, j:j + 1], ALU.add, ALU.bypass)
            kb.q("vector", "tensor_tensor_scan", scf[:, 0:T], wvt[:, :],
                 wvt[:, :], carryKV[:, j:j + 1], ALU.add, ALU.bypass)
            dv0 = kb.q("vector", "engine_nop", i=("DV", 1))
            ac0 = kb.q("scalar", "activation", scf[:, T:2 * T], bca[:, 0:T],
                       AF.Ln, bias=1e-6, w=[("DV", dv0)], i=("AC", 1))
            ac0 = kb.q("scalar", "activation", scf[:, T:2 * T],
                       scf[:, T:2 * T], AF.Exp, scale=-1.0, w=[("AC", ac0)],
                       i=("AC", 1))
            return kb.q("vector", "tensor_mul", kwr_row(j), scf[:, 0:T],
                        scf[:, T:2 * T], w=[("AC", ac0)], i=("DV", 1))

        evj = {}
        for j in range(KCH):
            par = vpar[j]
            for tt in range(NT):
                pp = tt % 2
                ej = kb.q("scalar", "copy",
                          vslab[j % 2][:, tt * TTK:(tt + 1) * TTK],
                          ps_m[pp][:, :], w=[("PE", mm_tiles(par, tt, pp))],
                          i=("AC", 1))
                evac[pp] = [("AC", ej)]
                evj[j] = ej
            mmdone[par] = kb.cnt["PE"]
            if j + 2 < KCH:
                vpar[j + 2] = stream_oc(32 + j + 2, j + 2)
            if j == KCH - 1:
                qpar0 = stream_oc(0, 0)
                qpar1 = stream_oc(1, 1)
            if j >= 1:
                v_chain(j - 1)
        dv = v_chain(KCH - 1)

        qsp = {}
        qpar = {}
        qpar[0] = qpar0
        qpar[1] = qpar1
        for j in range(KCH):
            par = qpar[j]
            for tt in range(NT):
                pp = tt % 2
                pe = mm_tiles(par, tt, pp)
                wv = [("PE", pe)]
                prev = qsp.get((j, tt - 2), qsp.get((j - 1, tt + 2)))
                if prev:
                    wv.append((f"SQ{tt % 2}", prev))
                dv = kb.q("vector", "tensor_copy", qtmp[:, pp * TTK:(pp + 1) * TTK],
                          ps_m[pp][:, :], w=wv, i=("DV", 1))
                a2 = kb.q("scalar", "activation", sqbf[:, 0:TTK],
                          qtmp[:, pp * TTK:(pp + 1) * TTK], AF.Square,
                          w=[("DV", dv)], i=("AC", 1))
                kb.q("tensor", "matmul", ps_st[tt][:, :], onescol[:, :],
                     sqbf[:, 0:TTK], w=[("AC", a2)], start=(j == 0),
                     stop=(j == KCH - 1),
                     i=("PE", 1) if j == KCH - 1 else None)
                evac[pp] = [("DV", dv), ("AC", a2)]
                qsp[(j, tt)] = kb.dma("gpsimd", q_sp[j, tt, :, :],
                                      qtmp[:, pp * TTK:(pp + 1) * TTK],
                                      sem=f"SQ{tt % 2}", w=[("DV", dv)])
            mmdone[par] = kb.cnt["PE"]
            if j + 2 < KCH:
                qpar[j + 2] = stream_oc(j + 2, j + 2)
        q_mm_done = kb.cnt["PE"]
        kb.q("vector", "_wait_only_", w=[("PE", q_mm_done)])
        for tt in range(NT):
            sig_rows(ps_st[tt], tt, bca[:, T:2 * T], ps_b2)
        ac_p = None
        ydv = {}
        qld = {}

        def q_reload(j):
            jp = j % 2
            half = wvt if jp else vtmp
            for tt in range(NT):
                wv = []
                if tt == 0:
                    wv = [("SQ0", qsp[(j, NT - 2)]), ("SQ1", qsp[(j, NT - 1)])]
                    if j < 2:
                        wv.append(("DV", kb.cnt["DV"]))
                    else:
                        wv.append(("DV", ydv[j - 2]))
                qld[j] = kb.dma("sync", half[:, tt * TTK:(tt + 1) * TTK],
                                q_sp[j, tt, :, :],
                                sem="LQ" if j % 2 == 0 else "LQ1", w=wv)

        q_reload(0)
        q_reload(1)
        for j in range(KCH):
            jp = j % 2
            half = wvt if jp else vtmp
            wv = [("LQ" if jp == 0 else "LQ1", qld[j]), ("PE", q_mm_done)]
            if ac_p:
                wv.append(("AC", ac_p))
            dv = kb.q("vector", "tensor_mul", scf[:, 0:T], half[:, :],
                      bca[:, T:2 * T], w=wv, i=("DV", 1))
            ydv[j] = dv
            ac_p = kb.q("scalar", "activation", scf[:, T:2 * T], scf[:, 0:T],
                        AF.Sigmoid, w=[("DV", dv)], i=("AC", 1))
            kb.q("vector", "tensor_mul", y_row(j), scf[:, T:2 * T], kwr_row(j),
                 w=[("AC", ac_p)])
            if j + 2 < KCH:
                q_reload(j + 2)
        kb.run_phase()
        _chk(nc, 5)

    def emit_pair_phase(wsrc, dst_sp, scaled, pre=None, side_gen_f=None):
        kb.new_phase()
        if pre:
            pre()
        from contextlib import ExitStack as _ES
        with _ES() as ph:
            S, P = _stack_alloc(nc, ph)
            ldb0 = S("ldb0", [128, KCH * 128], F32)
            ldb1 = S("ldb1", [128, KCH * 128], F32)
            wt0 = S("wt0", [128, KCH * 128], BF16)
            wt1 = S("wt1", [128, KCH * 128], BF16)
            ldb = (ldb0, ldb1)
            sgf = S("sgf", [128, 2 * TTK], F32)
            usb = S("usb", [128, 2 * TTK], BF16)
            midb = S("midb", [128, 2 * TTK], BF16)
            mhb = S("mhb", [128, 2 * TTK], BF16) if scaled else midb
            side_gen = None
            if side_gen_f:
                wdb = S("wdb", [128, 64 * 128], F32)
                wspare = S("wspare", [128, 2048], F32)
                side_gen = side_gen_f(wdb, wspare)
            ps_u0 = P("ps_u0", [128, TTK], F32)
            ps_u1 = P("ps_u1", [128, TTK], F32)
            ps_g0 = P("ps_g0", [128, TTK], F32)
            ps_g1 = P("ps_g1", [128, TTK], F32)
            ps_h0 = P("ps_h0", [128, TTK], F32)
            ps_h1 = P("ps_h1", [128, TTK], F32)
            wt = (wt0, wt1)
            ps_u = (ps_u0, ps_u1)
            ps_g = (ps_g0, ps_g1)
            ps_h = (ps_h0, ps_h1)
            ldg = {0: None, 1: None}
            mmdone = {0: None, 1: None}
            ev_u = {0: None, 1: None}
            ev_g = {0: None, 1: None}
            ev_h = {0: None, 1: None}
            mid_dv = {0: None, 1: None}
            last_msp = {0: None, 1: None}
            spq = {}
            pend_h = []
            convp = {0: None, 1: None}
            for p in range(64):
                for half in range(2):
                    oc = 2 * p + half
                    ldv = kb.dma("sync", ldb[half][:, :], wsrc[oc],
                                 sem=f"LW{half}",
                                 w=[("AC", convp[half])] if convp[half] else ())
                    wv = [(f"LW{half}", ldv)]
                    if mmdone[half]:
                        wv.append(("PE", mmdone[half]))
                    convp[half] = kb.q("scalar", "copy", wt[half][:, :],
                                       ldb[half][:, :], w=wv, i=("AC", 1))
                    ldg[half] = convp[half]
                for tt in range(NT):
                    pp = tt % 2
                    for half in range(2):
                        psx = ps_u[pp] if half == 0 else ps_g[pp]
                        evx = ev_u[pp] if half == 0 else ev_g[pp]
                        for kc in range(KCH):
                            wv = []
                            if kc == 0:
                                wv.append(("AC", ldg[half]))
                                if evx:
                                    wv.append(("DV" if half == 0 else "AC", evx))
                            pe = kb.q("tensor", "matmul", psx[:, :],
                                      wt[half][:, kc * 128:(kc + 1) * 128],
                                      y_row(kc)[:, tt * TTK:(tt + 1) * TTK],
                                      w=wv, start=(kc == 0), stop=(kc == KCH - 1),
                                      i=("PE", 1) if kc == KCH - 1 else None)
                        if half == 0:
                            pe_u = pe
                        else:
                            pe_g = pe
                    if pend_h:
                        mp2, gate = pend_h.pop(0)
                        wv = [("DV", gate)]
                        if ev_h[mp2]:
                            wv.append(("AC", ev_h[mp2]))
                        peh = kb.q("tensor", "matmul", ps_h[mp2][:, :],
                                   h128sb[:, :], midb[:, mp2 * TTK:(mp2 + 1) * TTK],
                                   w=wv, start=True, stop=True, i=("PE", 1))
                        wv2 = [("PE", peh)]
                        if last_msp[mp2]:
                            wv2.append((f"SM{mp2}", last_msp[mp2]))
                        ac = kb.q("scalar", "activation",
                                  mhb[:, mp2 * TTK:(mp2 + 1) * TTK],
                                  ps_h[mp2][:, :], AF.Copy, scale=C2,
                                  w=wv2, i=("AC", 1))
                        ev_h[mp2] = ac
                        pkey = hkeys.pop(0)
                        last_msp[mp2] = spq[pkey[1]] = kb.dma(
                            "gpsimd", dst_sp[pkey[0][0], pkey[0][1], :, :],
                            mhb[:, mp2 * TTK:(mp2 + 1) * TTK],
                            sem=f"SM{mp2}", w=[("AC", ac)])
                    S0 = pp * TTK
                    S1 = (pp + 1) * TTK
                    if scaled:
                        ev_g[pp] = kb.q("scalar", "activation", sgf[:, S0:S1],
                                        ps_g[pp][:, :], AF.Silu,
                                        scale=combg[:, p:p + 1],
                                        w=[("PE", pe_g), ("DV", mid_dv[pp])],
                                        i=("AC", 1))
                        ev_u[pp] = kb.q("vector", "tensor_scalar_mul",
                                        usb[:, S0:S1], ps_u[pp][:, :],
                                        combu[:, p:p + 1], w=[("PE", pe_u)],
                                        i=("DV", 1))
                    else:
                        ev_g[pp] = kb.q("scalar", "activation", sgf[:, S0:S1],
                                        ps_g[pp][:, :], AF.Silu,
                                        w=[("PE", pe_g), ("DV", mid_dv[pp])],
                                        i=("AC", 1))
                        ev_u[pp] = kb.q("vector", "tensor_copy", usb[:, S0:S1],
                                        ps_u[pp][:, :], w=[("PE", pe_u)],
                                        i=("DV", 1))
                    if side_gen is not None:
                        next(side_gen, None)
                    key = (p, tt)
                    wv = [("AC", ev_g[pp]), ("DV", ev_u[pp])]
                    if key in spq:
                        wv.append((f"SM{pp}", spq.pop(key)))
                    dv = kb.q("vector", "tensor_mul", midb[:, S0:S1],
                              usb[:, S0:S1], sgf[:, S0:S1], w=wv, i=("DV", 1))
                    mid_dv[pp] = dv
                    if side_gen is not None:
                        next(side_gen, None)
                        next(side_gen, None)
                    nkey = (p, tt + 2) if tt < 2 else (p + 1, tt - 2)
                    if scaled:
                        pend_h.append((pp, dv))
                        hkeys.append(((p, tt), nkey))
                    else:
                        spq[nkey] = kb.dma("gpsimd", dst_sp[p, tt, :, :],
                                           midb[:, S0:S1],
                                           sem=f"SM{pp}", w=[("DV", dv)])
                mmdone[0] = kb.cnt["PE"]
                mmdone[1] = kb.cnt["PE"]
            if side_gen is not None:
                for _ in side_gen:
                    pass
            while pend_h:
                mp2, gate = pend_h.pop(0)
                wv = [("DV", gate)]
                if ev_h[mp2]:
                    wv.append(("AC", ev_h[mp2]))
                peh = kb.q("tensor", "matmul", ps_h[mp2][:, :], h128sb[:, :],
                           midb[:, mp2 * TTK:(mp2 + 1) * TTK], w=wv,
                           start=True, stop=True, i=("PE", 1))
                wv2 = [("PE", peh)]
                if last_msp[mp2]:
                    wv2.append((f"SM{mp2}", last_msp[mp2]))
                ac = kb.q("scalar", "activation",
                          mhb[:, mp2 * TTK:(mp2 + 1) * TTK], ps_h[mp2][:, :],
                          AF.Copy, scale=C2, w=wv2, i=("AC", 1))
                ev_h[mp2] = ac
                pkey = hkeys.pop(0)
                last_msp[mp2] = spq[pkey[1]] = kb.dma(
                    "gpsimd", dst_sp[pkey[0][0], pkey[0][1], :, :],
                    mhb[:, mp2 * TTK:(mp2 + 1) * TTK],
                    sem=f"SM{mp2}", w=[("AC", ac)])
            kb.run_phase()

    hkeys = []

    def what_gen(wdb, wspare):
        sp_prev = None
        for oc in range(16):
            ldv = kb.dma("sync", wdb[:, :], wdownT[oc],
                         w=[("SPI", sp_prev)] if sp_prev else ())
            wv = [("LD", ldv)]
            dv = None
            for s in range(6):
                h = 2 ** s
                width = h * 128
                nblk = 64 // (2 * h)
                r = wdb[:, :].rearrange("p (b two w) -> p b two w", two=2,
                                        w=width)
                if width <= 2048:
                    g = max(1, 2048 // width)
                    for base in range(0, nblk, g):
                        gg = min(g, nblk - base)
                        a_v = r[:, base:base + gg, 0, :]
                        b_v = r[:, base:base + gg, 1, :]
                        sp_v = wspare[:, 0:gg * width].rearrange(
                            "p (b w) -> p b w", w=width)
                        kb.q("vector", "tensor_sub", sp_v, a_v, b_v, w=wv)
                        wv = ()
                        yield
                        kb.q("vector", "tensor_add", a_v, a_v, b_v)
                        yield
                        dv = kb.q("vector", "tensor_copy", b_v, sp_v,
                                  i=("DV", 1))
                        yield
                else:
                    for w0 in range(0, width, 2048):
                        a_v = r[:, 0, 0, w0:w0 + 2048]
                        b_v = r[:, 0, 1, w0:w0 + 2048]
                        kb.q("vector", "tensor_sub", wspare[:, :], a_v, b_v,
                             w=wv)
                        wv = ()
                        yield
                        kb.q("vector", "tensor_add", a_v, a_v, b_v)
                        yield
                        dv = kb.q("vector", "tensor_copy", b_v, wspare[:, :],
                                  i=("DV", 1))
                        yield
            sp_prev = kb.dma("gpsimd", wdh_sp[oc, :, :], wdb[:, :], sem="SPI",
                             w=[("DV", dv)])
            yield

    emit_pair_phase(wswT, psw_sp, scaled=False, side_gen_f=what_gen)

    def reload_xh():
        for tt in range(NT):
            for cg in range(4):
                dst = A_big[:, XQ0:XQ0 + KCH * T].rearrange(
                    "p (c t) -> p c t", c=KCH)[:, cg * 4:(cg + 1) * 4,
                                               tt * TTK:(tt + 1) * TTK]
                src = xh_sp[tt].rearrange("p (c t) -> p c t", c=KCH)[
                    :, cg * 4:(cg + 1) * 4, :]
                kb.dma("sync", dst, src)
        v = kb.cnt["LD"]
        for e in ("tensor", "vector", "scalar"):
            kb.q(e, "_wait_only_", w=[("LD", v)])

    emit_pair_phase(wugT, mid_sp, scaled=True, pre=reload_xh)

    def emit_big_contract(wsrc, cast, act_sp, writer, bc_builder, post=None):
        kb.new_phase()
        from contextlib import ExitStack as _ES
        with _ES() as ph:
            S, P = _stack_alloc(nc, ph)
            wt0 = S("wt0", [128, 64 * 128], BF16)
            wt1 = S("wt1", [128, 64 * 128], BF16)
            aux = S("aux", [128, 2 * TTK], F32)
            ob0 = S("ob0", [128, TTK], F32)
            ob1 = S("ob1", [128, TTK], F32)
            bcx = S("bcx", [128, T], F32)
            ps_o0 = P("ps_o0", [128, TTK], F32)
            ps_o1 = P("ps_o1", [128, TTK], F32)
            ps_bb = P("ps_bb", [128, TTK], F32)
            wt, ob, ps_o = (wt0, wt1), (ob0, ob1), (ps_o0, ps_o1)
            bc_builder(bcx, ps_bb)
            ldb0 = S("ldbA", [128, 1024], F32)
            ldb1 = S("ldbB", [128, 1024], F32)
            ldbp = (ldb0, ldb1)
            convq = {0: None, 1: None}
            ldg = {0: None, 1: None}
            mmdone = {0: None, 1: None}
            evo = {0: None, 1: None}
            obsp = {0: None, 1: None}
            for half in range(2):
                wv = [("PE", kb.cnt["PE"]), ("DV", kb.cnt["DV"]),
                      ("AC", kb.cnt["AC"])]
                wv += [(sn, kb.cnt[sn]) for sn in kb.SPILL_SEMS]
                for p in range(64):
                    srcv = act_sp[p].rearrange("tt p t -> p tt t")[
                        :, 2 * half:2 * half + 2, :]
                    dstv = A_big[:, p * 1024:(p + 1) * 1024].rearrange(
                        "p (tt t) -> p tt t", tt=2)
                    kb.dma("sync", dstv, srcv, w=wv if p == 0 else ())
                act_ld = kb.cnt["LD"]
                kb.q("tensor", "_wait_only_", w=[("LD", act_ld)])
                for o2 in range(16):
                    par = o2 % 2
                    if cast:
                        for pc in range(8):
                            bb = pc % 2
                            wv = [("AC", convq[bb])] if convq[bb] else []
                            ldv = kb.dma("sync", ldbp[bb][:, :],
                                         wsrc(o2)[:, pc * 8:(pc + 1) * 8, :],
                                         sem=f"LW{bb}", w=wv)
                            wv = [(f"LW{bb}", ldv)]
                            if mmdone[par] and pc < 4:
                                wv.append(("PE", mmdone[par]))
                            convq[bb] = kb.q(
                                "scalar", "copy",
                                wt[par][:, pc * 1024:(pc + 1) * 1024],
                                ldbp[bb][:, :], w=wv, i=("AC", 1))
                        ldg[par] = ("AC", convq[1])
                    else:
                        wv = [("PE", mmdone[par])] if mmdone[par] else []
                        ldv = kb.dma("sync", wt[par][:, :], wsrc(o2),
                                     sem=f"LW{par}", w=wv)
                        ldg[par] = (f"LW{par}", ldv)
                    for t2 in range(2):
                        tt = 2 * half + t2
                        pp = t2
                        for kc in range(64):
                            wv = []
                            if kc == 0:
                                wv.append(ldg[par])
                                if evo[pp]:
                                    wv.append(("DV", evo[pp]))
                            pe = kb.q("tensor", "matmul", ps_o[pp][:, :],
                                      wt[par][:, kc * 128:(kc + 1) * 128],
                                      A_big[:, kc * 1024 + t2 * TTK:
                                            kc * 1024 + (t2 + 1) * TTK],
                                      w=wv, start=(kc == 0), stop=(kc == 63),
                                      i=("PE", 1) if kc == 63 else None)
                        evo[pp] = writer(o2, tt, pp, ps_o[pp], ob[pp], pe, bcx,
                                         aux, obsp)
                    mmdone[par] = kb.cnt["PE"]
            if post:
                post()
            kb.run_phase()

    from contextlib import ExitStack as _ES8
    with _ES8() as ph8:
        msrow = ph8.enter_context(nc.sbuf_tensor("msrow", [1, T], F32))
        first_ld = {}

        def bc_ms(bcx, ps_bb):
            ldr = kb.dma("sync", msrow[:, :], rows_sp[1:2, :])
            for tt in range(NT):
                pe = kb.q("tensor", "matmul", ps_bb[:, :], ones1[:, :],
                          msrow[:, tt * TTK:(tt + 1) * TTK],
                          w=[("LD", ldr), ("DV", kb.cnt["DV"])],
                          start=True, stop=True, i=("PE", 1))
                kb.q("vector", "tensor_copy", bcx[:, tt * TTK:(tt + 1) * TTK],
                     ps_bb[:, :], w=[("PE", pe)], i=("DV", 1))

        def wout_writer(o2, tt, pp, psx, obx, pe, bcx, aux, obsp):
            wv = [("PE", pe)]
            if obsp[pp]:
                wv.append((f"SO{pp}", obsp[pp]))
            dv = kb.q("vector", "tensor_sub", obx[:, :], psx[:, :],
                      bcx[:, tt * TTK:(tt + 1) * TTK], w=wv, i=("DV", 1))
            obsp[pp] = kb.dma("gpsimd", proj_sp[o2, tt, :, :], obx[:, :],
                              sem=f"SO{pp}", w=[("DV", dv)])
            return dv

        emit_big_contract(lambda o2: woutT[o2], True, psw_sp, wout_writer,
                          bc_ms)

    from contextlib import ExitStack as _ES9
    with _ES9() as ph9:
        psrow = ph9.enter_context(nc.sbuf_tensor("psrow", [1, T], F32))
        prj = ph9.enter_context(nc.sbuf_tensor("prj", [128, 2 * TTK], F32))
        xin = ph9.enter_context(nc.sbuf_tensor("xin", [128, 2 * TTK], F32))

        def bc_1ps(bcx, ps_bb):
            ldr = kb.dma("sync", psrow[:, :], rows_sp[0:1, :])
            for tt in range(NT):
                pe = kb.q("tensor", "matmul", ps_bb[:, :], ones1[:, :],
                          psrow[:, tt * TTK:(tt + 1) * TTK],
                          w=[("LD", ldr), ("DV", kb.cnt["DV"])],
                          start=True, stop=True, i=("PE", 1))
                kb.q("vector", "tensor_copy", bcx[:, tt * TTK:(tt + 1) * TTK],
                     ps_bb[:, :], w=[("PE", pe)], i=("DV", 1))

        def down_writer(o2, tt, pp, psx, obx, pe, bcx, aux, obsp):
            wv = [("DV", evprev[pp])] if evprev[pp] else ()
            l1 = kb.dma("sync", prj[:, pp * TTK:(pp + 1) * TTK],
                        proj_sp[o2, tt, :, :], sem=f"LP{pp}", w=wv)
            l2 = kb.dma("sync", xin[:, pp * TTK:(pp + 1) * TTK],
                        x_own[o2 * 128:(o2 + 1) * 128, tt * TTK:(tt + 1) * TTK],
                        sem=f"LP{pp}")
            wv = [("PE", pe), (f"LP{pp}", l2)]
            if obsp[pp]:
                wv.append((f"SO{pp}", obsp[pp]))
            kb.q("vector", "scalar_tensor_tensor", obx[:, :], psx[:, :],
                 combdp[:, o2:o2 + 1], prj[:, pp * TTK:(pp + 1) * TTK],
                 ALU.mult, ALU.add, w=wv)
            kb.q("vector", "tensor_mul", aux[:, pp * TTK:(pp + 1) * TTK],
                 xin[:, pp * TTK:(pp + 1) * TTK],
                 bcx[:, tt * TTK:(tt + 1) * TTK], w=[(f"LP{pp}", l2)])
            dv = kb.q("vector", "tensor_add", obx[:, :], obx[:, :],
                      aux[:, pp * TTK:(pp + 1) * TTK], i=("DV", 1))
            evprev[pp] = dv
            obsp[pp] = kb.dma("gpsimd",
                              out_fm[o2 * 128:(o2 + 1) * 128,
                                     tt * TTK:(tt + 1) * TTK], obx[:, :],
                              sem=f"SO{pp}", w=[("DV", dv)])
            return dv

        evprev = {0: None, 1: None}

        def final_wait():
            kb.q("gpsimd", "_wait_only_",
                 w=[(sn, kb.cnt[sn]) for sn in kb.SPILL_SEMS])

        emit_big_contract(lambda o2: wdh_sp[o2], False, mid_sp, down_writer,
                          bc_1ps, post=final_wait)

    es.close()
    return nc


def _tile_w(wT):
    K, O = wT.shape
    kc, oc = K // 128, O // 128
    t = wT.reshape(kc, 128, oc, 128).transpose(2, 1, 0, 3)
    return np.ascontiguousarray(t)


def _prep_inputs(L0_attn_x, L0_attn_w_qkv, L0_attn_w_swiglu, L0_attn_w_out,
                 L0_ffn_SU_ug, L0_ffn_SU_dp, L0_ffn_Wscale_ug, L0_ffn_Wscale_dp,
                 L0_ffn_W_up, L0_ffn_W_gate, L0_ffn_W_down):
    f = np.ascontiguousarray
    x = np.asarray(L0_attn_x, np.float32)
    xt = np.ascontiguousarray(x.transpose(0, 2, 1))

    wqkvT = _tile_w(f(np.asarray(L0_attn_w_qkv, np.float32).T))
    wsw = np.asarray(L0_attn_w_swiglu, np.float32)
    il = np.stack([wsw[:8192].T.reshape(D, 64, 128),
                   wsw[8192:].T.reshape(D, 64, 128)], axis=2).reshape(D, 16384)
    wswT = _tile_w(f(il))
    wup = np.asarray(L0_ffn_W_up, np.float32)
    wga = np.asarray(L0_ffn_W_gate, np.float32)
    il2 = np.stack([wup.T.reshape(D, 64, 128), wga.T.reshape(D, 64, 128)],
                   axis=2).reshape(D, 16384)
    wugT = _tile_w(f(il2))
    woutT = _tile_w(f(np.asarray(L0_attn_w_out, np.float32).T))
    wdownT = f(np.asarray(L0_ffn_W_down, np.float32).T
               .reshape(64, 128, 16, 128).transpose(2, 1, 0, 3))

    c2v = lambda vv, n: f(np.asarray(vv, np.float32).reshape(n, 128).T)
    shared = dict(
        wqkvT=wqkvT, wswT=wswT, wugT=wugT, woutT=woutT, wdownT=wdownT,
        su_ug2=c2v(L0_ffn_SU_ug, 16), su_dp2=c2v(L0_ffn_SU_dp, 64),
        wscu2=c2v(np.asarray(L0_ffn_Wscale_ug, np.float32)[:8192], 64),
        wscg2=c2v(np.asarray(L0_ffn_Wscale_ug, np.float32)[8192:], 64),
        wscdp2=c2v(L0_ffn_Wscale_dp, 16),
        h128in=_had(128).astype(ml_dtypes.bfloat16))
    in_maps = []
    for c in range(8):
        b, l = divmod(c, 4)
        x_own = f(xt[b][:, l * T:(l + 1) * T])
        x_for = np.zeros((3, D, T), np.float32)
        for j in range(l):
            x_for[j] = xt[b][:, j * T:(j + 1) * T]
        cm = np.zeros((128, 4), np.float32)
        cm[:, :l] = 1.0
        m = dict(shared)
        m.update(x_own=x_own, x_for=f(x_for), cmask=cm)
        in_maps.append(m)
    return in_maps


def kernel(**inputs):
    in_maps = _prep_inputs(**inputs)
    nc = build()
    res = run_bass_kernel_spmd(nc, in_maps, core_ids=list(range(8)))
    out = np.empty((2, 8192, D), np.float32)
    for c in range(8):
        b, l = divmod(c, 4)
        out[b, l * T:(l + 1) * T, :] = res.results[c]["out_fm"].T
    return out
